# version 05 (final)
# baseline (speedup 1.0000x reference)
import math
import contextlib
import numpy as np
import ml_dtypes
import concourse.bass as bass
import concourse.mybir as mybir
from concourse.bass_utils import run_bass_kernel_spmd
from concourse.ap import AP

F32 = mybir.dt.float32
BF16 = mybir.dt.bfloat16
AF = mybir.ActivationFunctionType
OP = mybir.AluOpType
BFD = ml_dtypes.bfloat16

B, T, D, H, HD = 2, 8192, 1024, 16, 64
NCORE, G = 8, 4
TL = T // G
NTC = TL // 128
NDM = D // 128
MYH, MYE = 4, 256
NKT = T // 128
QHW = 512
NQH = T // QHW
SHIFT = 16.0
EPS = 1e-5
EPS1 = float(T) * float(T) * EPS
SAT = 778
WSTRIP = 2707
MLO = 6774
E_DN, E_UP = 512, 2706
GELU_C = 0.7978845608028654
GELU_A = 0.044715

CN = {"eps": 0, "eps1": 1, "nshift": 2, "inv1024": 3, "one": 4}
CONSTS_H = np.zeros((128, 8), np.float32)
CONSTS_H[:, CN["eps"]] = EPS
CONSTS_H[:, CN["eps1"]] = EPS1
CONSTS_H[:, CN["nshift"]] = -SHIFT
CONSTS_H[:, CN["inv1024"]] = 1.0 / 1024.0
CONSTS_H[:, CN["one"]] = 1.0


def _bucket_np(rel):
    NB, MAXD = 320, 800
    nb = NB // 2
    relb = (rel > 0).astype(np.int64) * nb
    rp = np.abs(rel)
    mx = nb // 2
    small = rp < mx
    with np.errstate(divide="ignore"):
        rl = np.log(np.maximum(rp, 1).astype(np.float32) / mx) / math.log(MAXD / mx)
    rl = (mx + rl * (nb - mx)).astype(np.int64)
    rl = np.minimum(rl, nb - 1)
    return relb + np.where(small, rp, rl)


_ONEHOT_CACHE = None


def _onehot():
    global _ONEHOT_CACHE
    if _ONEHOT_CACHE is None:
        m = np.arange(16383) - (T - 1)
        bk = _bucket_np(m)
        oh = np.zeros((384, 16383), np.float32)
        oh[bk, np.arange(16383)] = 1.0
        _ONEHOT_CACHE = oh.astype(BFD)
        assert (bk[(m >= SAT)] == bk[-1]).all() and not (bk[(m >= SAT - 1)] == bk[-1]).all()
        assert (bk[(m <= -SAT)] == bk[0]).all()
    return _ONEHOT_CACHE


_DETECT = False

IDN_H = np.eye(128, dtype=np.float32)
ONES_H = np.ones((128, 4), np.float32)
ONES_H[:, 2:4] = 0.0


class Prog:
    ENGINES = ("sync", "gpsimd", "vector", "scalar", "tensor")

    def __init__(self, nc):
        self.nc = nc
        self.q = {e: [] for e in self.ENGINES}
        self.semh = {}
        self.semc = {}
        self.es = contextlib.ExitStack()

    def sem(self, name):
        if name not in self.semh:
            self.semh[name] = self.es.enter_context(self.nc.semaphore(name))
            self.semc[name] = 0
        return name

    def c(self, name):
        return self.semc[name]

    def sb(self, name, shape, dt, stack=None):
        return (stack or self.es).enter_context(self.nc.sbuf_tensor(name, shape, dt))

    def ps(self, name, shape, dt=F32, stack=None):
        return (stack or self.es).enter_context(self.nc.psum_tensor(name, shape, dt))

    def emit(self, eng, build, waits=(), incs=()):
        for s, k in incs:
            self.semc[s] += k
        wl = [(self.semh[s], v) for s, v in waits]
        il = [(self.semh[s], k) for s, k in incs]

        def fn(e):
            for h, v in wl:
                e.wait_ge(h, v)
            inst = build(e)
            if il:
                assert inst is not None
                for h, k in il:
                    inst = inst.then_inc(h, k)
        self.q[eng].append(fn)

    def barrier_counts(self):
        return dict(self.semc)

    def wait_all(self, eng, snap):
        self.emit(eng, lambda e: None, waits=[(s, v) for s, v in snap.items() if v > 0])

    def finalize(self):
        blk = self.es.enter_context(self.nc.Block())

        def runner(lst):
            def _(e):
                for f in lst:
                    f(e)
            return _
        blk.sync(runner(self.q["sync"]))
        blk.gpsimd(runner(self.q["gpsimd"]))
        blk.vector(runner(self.q["vector"]))
        blk.scalar(runner(self.q["scalar"]))
        blk.tensor(runner(self.q["tensor"]))

    def close(self):
        self.es.close()


def load_consts(P, env):
    c = P.sb("consts", [128, 8], F32)
    idn_f = P.sb("idn_f", [128, 128], F32)
    idn = P.sb("idn", [128, 128], BF16)
    ones_f = P.sb("ones_f", [128, 4], F32)
    ones = P.sb("ones", [128, 4], BF16)
    s = P.sem("s_cst")
    P.emit("sync", lambda e, d=env["CONSTS"], t=c: e.dma_start(out=t[:, :], in_=d[:, :]), incs=[(s, 16)])
    P.emit("sync", lambda e, d=env["IDN"], t=idn_f: e.dma_start(out=t[:, :], in_=d[:, :]), incs=[(s, 16)])
    P.emit("sync", lambda e, d=env["ONES"], t=ones_f: e.dma_start(out=t[:, :], in_=d[:, :]), incs=[(s, 16)])
    P.emit("vector", lambda e, a=idn_f, b=idn: e.tensor_copy(out=b[:, :], in_=a[:, :]),
           waits=[(s, 48)], incs=[(s, 1)])
    P.emit("vector", lambda e, a=ones_f, b=ones: e.tensor_copy(out=b[:, :], in_=a[:, :]), incs=[(s, 1)])
    env.update(consts=c, idn=idn, idn_f=idn_f, ones=ones, s_cst=s, cst_target=P.c(s))
    return env


def cc(env, name):
    return env["consts"][:, CN[name]:CN[name] + 1]


def emit_ln(P, pre, src_sb, nchunk, dst_bf, dst_f32=None, stack=None, eps_ap=None):
    s = P.sem(pre + "_ln")
    sq = P.sem(pre + "_sq")
    stat = P.sb(pre + "_stat", [128, 7 * 16], F32, stack)
    P._lnstat = stat
    dump = [P.sb(pre + f"_dump{i}", [128, 1024], BF16, stack) for i in range(2)]
    for ci in range(nchunk):
        sl = slice(ci * 1024, (ci + 1) * 1024)
        P.emit("scalar", lambda e, x=src_sb, sl=sl, dp=dump[ci % 2]:
               e.activation(out=dp[:, :], in_=x[:, sl], func=AF.Square),
               waits=([(s, 2 * (ci - 1))] if ci >= 2 else []), incs=[(sq, 1)])
        P.emit("vector", lambda e, t=stat, x=src_sb, ci=ci, sl=sl:
               e.reduce_sum(t[:, ci:ci + 1], x[:, sl], axis=mybir.AxisListType.X),
               incs=[(s, 1)])
        P.emit("vector", lambda e, t=stat, ci=ci, dp=dump[ci % 2]:
               e.reduce_sum(t[:, 16 + ci:16 + ci + 1], dp[:, :], axis=mybir.AxisListType.X),
               waits=[(sq, ci + 1)], incs=[(s, 1)])
    n2 = 2 * nchunk
    P.emit("scalar", lambda e, t=stat, n=nchunk:
           e.activation(out=t[:, 32:32 + n], in_=t[:, 0:n], func=AF.Identity,
                        scale=1.0 / 1024.0),
           waits=[(s, n2)])
    P.emit("scalar", lambda e, t=stat, n=nchunk:
           e.activation(out=t[:, 48:48 + n], in_=t[:, 16:16 + n], func=AF.Identity,
                        scale=1.0 / 1024.0))
    P.emit("scalar", lambda e, t=stat, n=nchunk:
           e.activation(out=t[:, 64:64 + n], in_=t[:, 32:32 + n], func=AF.Square),
           incs=[(s, 1)])
    P.emit("vector", lambda e, t=stat, n=nchunk:
           e.tensor_tensor(out=t[:, 80:80 + n], in0=t[:, 48:48 + n], in1=t[:, 64:64 + n], op=OP.subtract),
           waits=[(s, n2 + 1)], incs=[(s, 1)])
    P.emit("scalar", lambda e, t=stat, n=nchunk, ec=eps_ap:
           e.activation(out=t[:, 48:48 + n], in_=t[:, 80:80 + n], func=AF.Ln, bias=ec),
           waits=[(s, n2 + 2)], incs=[(s, 1)])
    P.emit("scalar", lambda e, t=stat, n=nchunk:
           e.activation(out=t[:, 96:96 + n], in_=t[:, 48:48 + n], func=AF.Exp, scale=-0.5),
           incs=[(s, 1)])
    base = n2 + 4
    first = [True]
    for ci in range(nchunk):
        sl = slice(ci * 1024, (ci + 1) * 1024)
        if dst_bf is not None:
            P.emit("vector", lambda e, t=stat, x=src_sb, d=dst_bf, ci=ci, sl=sl:
                   e.tensor_scalar(out=d[:, sl], in0=x[:, sl], scalar1=t[:, 32 + ci:32 + ci + 1],
                                   scalar2=t[:, 96 + ci:96 + ci + 1], op0=OP.subtract, op1=OP.mult),
                   waits=([(s, base)] if first[0] else []), incs=[(s, 1)])
            first[0] = False
        if dst_f32 is not None:
            P.emit("vector", lambda e, t=stat, x=src_sb, d=dst_f32, ci=ci, sl=sl:
                   e.tensor_scalar(out=d[:, sl], in0=x[:, sl], scalar1=t[:, 32 + ci:32 + ci + 1],
                                   scalar2=t[:, 96 + ci:96 + ci + 1], op0=OP.subtract, op1=OP.mult),
                   waits=([(s, base)] if first[0] else []), incs=[(s, 1)])
            first[0] = False
    return s, P.c(s)


def emit_transpose_16(P, pre, src_bf, dst_T, idn, stack=None, nchunk=16, ndm=NDM):
    s = P.sem(pre + "_tr")
    pst = [P.ps(pre + f"_pst{i}", [128, 128], BF16, stack) for i in range(2)]
    TW = nchunk * 128
    n = 0
    for c in range(nchunk):
        for j in range(ndm):
            par = n % 2
            w = []
            if n >= 2:
                w.append((s, 2 * n - 1))
            P.emit("tensor", lambda e, ps=pst[par], x=src_bf, c=c, j=j, idn=idn:
                   e.transpose(ps[:, :], x[:, c * 1024 + j * 128: c * 1024 + (j + 1) * 128], idn[:, :]),
                   waits=w, incs=[(s, 1)])
            P.emit("vector", lambda e, ps=pst[par], d=dst_T, c=c, j=j, TW=TW:
                   e.tensor_copy(out=d[:, j * TW + c * 128: j * TW + (c + 1) * 128], in_=ps[:, :]),
                   waits=[(s, 2 * n + 1)], incs=[(s, 1)])
            n += 1
    return s, P.c(s)


def build_stageA(debug=False):
    nc = bass.Bass("TRN2", target_bir_lowering=False, detect_race_conditions=_DETECT)
    env = {}
    env["X"] = nc.dram_tensor("X", [TL, D], F32, kind="ExternalInput")
    env["RELP"] = nc.dram_tensor("RELP", [384, MYH], F32, kind="ExternalInput")
    env["OH"] = nc.dram_tensor("OH", [384, 16383], BF16, kind="ExternalInput")
    env["CONSTS"] = nc.dram_tensor("CONSTS", [128, 8], F32, kind="ExternalInput")
    env["IDN"] = nc.dram_tensor("IDN", [128, 128], F32, kind="ExternalInput")
    env["ONES"] = nc.dram_tensor("ONES", [128, 4], F32, kind="ExternalInput")
    o_lnT = nc.dram_tensor("o_lnT", [D, TL], BF16, kind="ExternalOutput")
    o_tab = nc.dram_tensor("o_tab", [MYH, 16384], BF16, kind="ExternalOutput")

    P = Prog(nc)
    load_consts(P, env)
    s_rld = P.sem("s_rld"); s_rcv = P.sem("s_rcv")
    s_xld = P.sem("s_xld")
    s_oh = [P.sem("s_oh0"), P.sem("s_oh1")]
    s_mm = P.sem("s_mm")
    s_ev = P.sem("s_ev")
    s_ts = [P.sem("s_ts0"), P.sem("s_ts1")]
    s_out = P.sem("s_out")

    rel_f = P.sb("rel_f", [128, 3 * MYH], F32)
    rel_b = P.sb("rel_b", [128, 3 * MYH], BF16)
    P.emit("sync", lambda e, d=env["RELP"], t=rel_f:
           e.dma_start(out=t[:, :], in_=d.ap().rearrange("(c p) j -> p c j", p=128)),
           incs=[(s_rld, 16)])
    P.emit("vector", lambda e, a=rel_f, b=rel_b: e.tensor_copy(out=b[:, :], in_=a[:, :]),
           waits=[(s_rld, 16)], incs=[(s_rcv, 1)])
    oh_sb = [P.sb(f"oh{i}", [128, 3 * 512], BF16) for i in range(2)]
    tstage = [P.sb(f"tstage{i}", [MYH, 512 + 16], BF16) for i in range(2)]
    pst_tab = P.ps("pst_tab", [MYH, 512], F32)
    P.emit("vector", lambda e, t=tstage[1]: e.memset(t[:, :], 0.0), incs=[(s_ev, 1)])
    P.emit("vector", lambda e, t=tstage[0]: e.memset(t[:, :], 0.0), incs=[(s_ev, 1)])
    ntile = 32
    for nt in range(ntile):
        par = nt % 2
        w = min(512, 16383 - nt * 512)
        for ch in range(3):
            P.emit("sync", lambda e, d=env["OH"], t=oh_sb[par], ch=ch, nt=nt, w=w:
                   e.dma_start(out=t[0:128, ch * 512:ch * 512 + w],
                               in_=d[ch * 128:(ch + 1) * 128, nt * 512:nt * 512 + w]),
                   waits=([(s_mm, nt - 1)] if (nt >= 2 and ch == 0) else []),
                   incs=[(s_oh[par], 16)])
        for ch in range(3):
            P.emit("tensor", lambda e, ps=pst_tab, r=rel_b, t=oh_sb[par], ch=ch, w=w:
                   e.matmul(ps[0:MYH, 0:w], r[:, ch * MYH:(ch + 1) * MYH],
                            t[0:128, ch * 512:ch * 512 + w],
                            start=(ch == 0), stop=(ch == 2)),
                   waits=([(s_oh[par], 48 * (nt // 2 + 1)), (s_rcv, 1), (s_ev, 2 + nt)]
                          if ch == 0 else []),
                   incs=([(s_mm, 1)] if ch == 2 else []))
        P.emit("vector", lambda e, ps=pst_tab, t=tstage[par], w=w:
               e.tensor_copy(out=t[0:MYH, 0:w], in_=ps[0:MYH, 0:w]),
               waits=[(s_mm, nt + 1)] + ([(s_ts[par], 16 * (nt // 2))] if nt >= 2 else []),
               incs=[(s_ev, 1)])
        wall = w if nt < ntile - 1 else (w + 1)
        P.emit("gpsimd", lambda e, d=o_tab, t=tstage[par], nt=nt, wall=wall:
               e.dma_start(out=d[:, nt * 512:nt * 512 + wall], in_=t[0:MYH, 0:wall]),
               waits=[(s_ev, 3 + nt)], incs=[(s_ts[par], 16)])

    x_sb = P.sb("x_sb", [128, NTC * 1024], F32)
    ln_bf = P.sb("ln_bf", [128, NTC * 1024], BF16)
    lnT = P.sb("lnT", [128, NDM * TL], BF16)
    P.emit("sync", lambda e, d=env["X"], t=x_sb:
           e.dma_start(out=t[:, :], in_=d.ap().rearrange("(c p) d -> p c d", p=128)),
           incs=[(s_xld, 16)])
    P.emit("vector", lambda e: None,
           waits=[(s_xld, 16), (env["s_cst"], env["cst_target"])])
    P.emit("scalar", lambda e: None, waits=[(s_xld, 16), (env["s_cst"], env["cst_target"])])
    sl, _ = emit_ln(P, "ln1", x_sb, NTC, ln_bf, eps_ap=cc(env, "eps"))
    snap = P.c(sl)
    if debug:
        o_ln = nc.dram_tensor("o_ln", [TL, D], BF16, kind="ExternalOutput")
        o_stat = nc.dram_tensor("o_stat", [128, 112], F32, kind="ExternalOutput")
        P.emit("gpsimd", lambda e, d=o_ln, t=ln_bf:
               e.dma_start(out=d.ap().rearrange("(c p) d -> p c d", p=128),
                           in_=t[:, :].rearrange("p (c d) -> p c d", c=NTC)),
               waits=[(sl, snap)], incs=[(s_out, 16)])
        stat_t = P._lnstat
        P.emit("gpsimd", lambda e, d=o_stat, t=stat_t:
               e.dma_start(out=d[:, :], in_=t[:, :]),
               waits=[(sl, snap)], incs=[(s_out, 16)])
    P.emit("tensor", lambda e: None, waits=[(sl, snap), (env["s_cst"], env["cst_target"])])
    st, _ = emit_transpose_16(P, "ln1", ln_bf, lnT, env["idn"])
    P.emit("gpsimd", lambda e, d=o_lnT, t=lnT:
           e.dma_start(out=d.ap().rearrange("(j p) t -> p j t", p=128),
                       in_=t[:, :].rearrange("p (j t) -> p j t", j=NDM)),
           waits=[(st, P.c(st))], incs=[(s_out, 16)])
    P.emit("gpsimd", lambda e: None,
           waits=[(s_out, P.c(s_out)), (s_ts[0], P.c(s_ts[0])), (s_ts[1], P.c(s_ts[1]))])
    P.finalize()
    return nc, P


def region_of(qh, kt):
    E = 1417 + 128 * kt - 512 * qh
    if E <= E_DN:
        return "dn", 511
    if E >= E_UP:
        return "up", 2706
    return "band", E


def build_stageB(debug=False, phases=4):
    nc = bass.Bass("TRN2", target_bir_lowering=False, detect_race_conditions=_DETECT)
    env = {}
    ti = lambda n, shp, dt=F32: nc.dram_tensor(n, shp, dt, kind="ExternalInput")
    env["LNT"] = ti("LNT", [D, T], BF16)
    env["LNTG"] = ti("LNTG", [MYE, T], BF16)
    env["TAB"] = ti("TAB", [MYH, 16384], BF16)
    env["QWT"] = ti("QWT", [D, MYE])
    env["KWT"] = ti("KWT", [D, MYE])
    env["VWT"] = ti("VWT", [D, MYE])
    env["QB"] = ti("QB", [MYE])
    env["KB"] = ti("KB", [MYE])
    env["VB"] = ti("VB", [MYE])
    env["GWT"] = ti("GWT", [64, 8])
    env["GB"] = ti("GB", [8])
    env["GC"] = ti("GC", [MYH, 1])
    env["OWT"] = ti("OWT", [MYE, D])
    env["CONSTS"] = ti("CONSTS", [128, 8])
    env["IDN"] = ti("IDN", [128, 128])
    env["ONES"] = ti("ONES", [128, 4])
    OPART = nc.dram_tensor("OPART", [T, D], F32, kind="ExternalOutput")
    GATE = nc.dram_tensor("GATE", [MYH, T], F32)
    FUP = nc.dram_tensor("FUP", [MYH, T], F32)
    FDN = nc.dram_tensor("FDN", [MYH, T], F32)
    RB = nc.dram_tensor("RB", [4, QHW], F32)
    ik = "ExternalOutput" if debug else "Internal"
    QTD = nc.dram_tensor("o_qT", [128, 2 * T], BF16, kind=ik)
    KTD = nc.dram_tensor("o_kT", [128, 2 * T], BF16, kind=ik)
    VTD = nc.dram_tensor("o_vT", [128, 2 * T], BF16, kind=ik)
    CTXD = nc.dram_tensor("o_ctxT", [128, 2 * T], BF16, kind=ik)

    P = Prog(nc)
    load_consts(P, env)
    sgm = P.sb("sgm", [128, 8 * 64], F32)
    gccol = P.sb("gccol", [MYH, 1], F32)

    ph1 = contextlib.ExitStack()
    s_wld = P.sem("s_wld"); s_wcv = P.sem("s_wcv"); s_wst = P.sem("s_wst")
    wstage = P.sb("wstage", [128, 2048], F32, ph1)
    qw = P.sb("qw", [128, 2048], BF16, ph1)
    kw = P.sb("kw", [128, 2048], BF16, ph1)
    vw = P.sb("vw", [128, 2048], BF16, ph1)
    for i, (srcn, dst) in enumerate([("QWT", qw), ("KWT", kw), ("VWT", vw)]):
        P.emit("sync", lambda e, d=env[srcn], t=wstage:
               e.dma_start(out=t[:, :], in_=d.ap().rearrange("(c p) e -> p c e", p=128)),
               waits=([(s_wcv, i)] if i else []), incs=[(s_wst, 16)])
        P.emit("vector", lambda e, a=wstage, b=dst: e.tensor_copy(out=b[:, :], in_=a[:, :]),
               waits=[(s_wst, 16 * (i + 1))], incs=[(s_wcv, 1)])
    qb = P.sb("qb", [128, 2], F32, ph1)
    kb = P.sb("kb", [128, 2], F32, ph1)
    vb = P.sb("vb", [128, 2], F32, ph1)
    gwf = P.sb("gwf", [128, 8], F32, ph1)
    gw4f = P.sb("gw4f", [128, 2], F32, ph1)
    gw4 = P.sb("gw4", [128, 2], BF16, ph1)
    gb_bc = P.sb("gb_bc", [128, 8], F32, ph1)
    gbs = P.sb("gbs", [128, 2], F32, ph1)
    for srcn, dst in [("QB", qb), ("KB", kb), ("VB", vb)]:
        P.emit("sync", lambda e, d=env[srcn], t=dst:
               e.dma_start(out=t[:, :], in_=d.ap().rearrange("(c p) -> p c", p=128),
                           allow_slow_non_contiguous=True),
               incs=[(s_wld, 16)])
    P.emit("sync", lambda e, d=env["GWT"], t=gwf: e.dma_start(out=t[0:64, :], in_=d[:, :]),
           incs=[(s_wld, 16)])
    P.emit("sync", lambda e, d=env["GWT"], t=gwf: e.dma_start(out=t[64:128, :], in_=d[:, :]),
           incs=[(s_wld, 16)])
    P.emit("sync", lambda e, d=env["GB"], t=gb_bc:
           e.dma_start(out=t[:, :], in_=d.ap().unsqueeze(0).partition_broadcast(128)),
           incs=[(s_wld, 16)])
    P.emit("sync", lambda e, d=env["GC"], t=gccol:
           e.dma_start(out=t[0:MYH, :], in_=d[:, :]),
           incs=[(s_wld, 16)])
    wld_t = P.c(s_wld)
    P.emit("vector", lambda e, a=gwf, b=gw4f:
           e.reduce_sum(b[:, 0:1], a[:, 0:4], axis=mybir.AxisListType.X),
           waits=[(s_wld, wld_t)], incs=[(s_wcv, 1)])
    P.emit("vector", lambda e, a=gwf, b=gw4f:
           e.reduce_sum(b[:, 1:2], a[:, 4:8], axis=mybir.AxisListType.X),
           incs=[(s_wcv, 1)])
    red_t = P.c(s_wcv)
    P.emit("scalar", lambda e, a=gw4f, b=gw4:
           e.activation(out=b[:, :], in_=a[:, :], func=AF.Identity),
           waits=[(s_wcv, red_t)], incs=[(s_wcv, 1)])
    P.emit("vector", lambda e, t=qb:
           e.tensor_scalar(out=t[:, :], in0=t[:, :], scalar1=0.125, scalar2=None, op0=OP.mult),
           incs=[(s_wcv, 1)])
    P.emit("vector", lambda e, t=gbs, g=gb_bc:
           e.reduce_sum(t[:, 0:1], g[:, 0:4], axis=mybir.AxisListType.X), incs=[(s_wcv, 1)])
    P.emit("vector", lambda e, t=gbs, g=gb_bc:
           e.reduce_sum(t[:, 1:2], g[:, 4:8], axis=mybir.AxisListType.X), incs=[(s_wcv, 1)])
    wcv_t = P.c(s_wcv)

    s_lnt2 = [P.sem("s_lnt0"), P.sem("s_lnt1")]
    s_mm = P.sem("s_mm"); s_pe = P.sem("s_pe")
    s_pd4 = [P.sem(f"s_pd{i}") for i in range(4)]
    s_gm = P.sem("s_gm"); s_sig = P.sem("s_sig"); s_gout = P.sem("s_gout")
    lnt_sb = [P.sb(f"lnt{i}", [128, NDM * 512], BF16, ph1) for i in range(2)]
    lntg_sb = [P.sb(f"lntg{i}", [128, 2 * 512], BF16, ph1) for i in range(2)]
    pstg = [P.sb(f"pstg{i}", [128, 512], BF16, ph1) for i in range(4)]
    pp = [P.ps(f"pp{i}", [128, 512], F32, ph1) for i in range(2)]
    pg2 = [P.ps(f"pg{i}", [128, 8], F32, ph1) for i in range(2)]
    NT512 = T // 512
    gidx = 0
    for tt in range(NT512):
        par = tt % 2
        w = []
        if tt >= 2:
            w = [(s_mm, 6 * (tt - 1)), (s_gm, 4 * (tt - 1))]
        P.emit("sync", lambda e, d=env["LNT"], t=lnt_sb[par], tt=tt:
               e.dma_start(out=t[:, :], in_=d[:, tt * 512:(tt + 1) * 512]
                           .rearrange("(c p) t -> p c t", p=128)),
               waits=w, incs=[(s_lnt2[par], 16)])
        P.emit("sync", lambda e, d=env["LNTG"], t=lntg_sb[par], tt=tt:
               e.dma_start(out=t[:, :], in_=d[:, tt * 512:(tt + 1) * 512]
                           .rearrange("(c p) t -> p c t", p=128)),
               incs=[(s_lnt2[par], 16)])
        for wsb, bias, scale, dstd in [(qw, qb, 0.125, QTD), (kw, kb, 1.0, KTD), (vw, vb, 1.0, VTD)]:
            for ech in range(2):
                ppar = gidx % 2
                spar = gidx % 4
                for c in range(NDM):
                    wt = []
                    if c == 0:
                        wt = [(s_lnt2[par], 32 * (tt // 2 + 1)), (s_wcv, wcv_t)]
                        if gidx >= 2:
                            wt.append((s_pe, gidx - 1))
                    P.emit("tensor", lambda e, ps=pp[ppar], wsb=wsb, c=c, ech=ech, t=lnt_sb[par]:
                           e.matmul(ps[:, :], wsb[:, c * 256 + ech * 128:c * 256 + (ech + 1) * 128],
                                    t[:, c * 512:(c + 1) * 512],
                                    start=(c == 0), stop=(c == NDM - 1)),
                           waits=wt, incs=([(s_mm, 1)] if c == NDM - 1 else []))
                wpe = [(s_mm, gidx + 1)]
                if gidx >= 4:
                    wpe.append((s_pd4[gidx % 4], 16 * (gidx // 4)))
                P.emit("scalar", lambda e, ps=pp[ppar], st=pstg[spar], bias=bias, scale=scale, ech=ech:
                       e.activation(out=st[:, :], in_=ps[:, :], func=AF.Identity,
                                    bias=bias[:, ech:ech + 1], scale=scale),
                       waits=wpe, incs=[(s_pe, 1)])
                P.emit("gpsimd", lambda e, d=dstd, st=pstg[spar], ech=ech, tt=tt:
                       e.dma_start(out=d[:, ech * T + tt * 512:ech * T + (tt + 1) * 512],
                                   in_=st[:, :]),
                       waits=[(s_pe, gidx + 1)], incs=[(s_pd4[gidx % 4], 16)])
                gidx += 1
        for sub in range(4):
            sg = tt * 4 + sub
            pg = pg2[sg % 2]
            for h in range(MYH):
                hl, ro = h // 2, (h % 2) * 64
                for ty in range(2):
                    wt = []
                    if h == 0 and ty == 0:
                        wt = [(s_wcv, wcv_t)]
                        if sg >= 2:
                            wt.append((s_sig, 2 * (sg - 1)))
                    P.emit("tensor", lambda e, ps=pg, t=lntg_sb[par], hl=hl, ro=ro,
                           sub=sub, h=h, ty=ty, g4=gw4:
                           e.matmul(ps[:, ty * 4 + h:ty * 4 + h + 1],
                                    t[ro:ro + 64, hl * 512 + sub * 128:hl * 512 + sub * 128 + 128],
                                    g4[ro:ro + 64, ty:ty + 1], start=True, stop=True),
                           waits=wt, incs=([(s_gm, 1)] if (h, ty) == (MYH - 1, 1) else []))
            for ty in range(2):
                P.emit("scalar", lambda e, ps=pg, t=sgm, sg=sg, ty=ty, b=gbs:
                       e.activation(out=t[:, sg * 8 + ty * 4:sg * 8 + ty * 4 + 4],
                                    in_=ps[:, ty * 4:ty * 4 + 4], func=AF.Sigmoid,
                                    bias=b[:, ty:ty + 1]),
                       waits=([(s_gm, sg + 1)] if ty == 0 else []), incs=[(s_sig, 1)])
    pe_t = P.c(s_pe)
    pd_t = [P.c(s) for s in s_pd4]
    sig_t = P.c(s_sig)
    ph1.close()
    if phases <= 1:
        P.emit("gpsimd", lambda e: None,
               waits=[(s_pe, pe_t), (s_sig, sig_t)] + [(s, v) for s, v in zip(s_pd4, pd_t)])
        P.finalize()
        return nc, P

    ph2 = contextlib.ExitStack()
    s_gt = P.sem("s_gt"); s_gd = P.sem("s_gd"); s_cu = P.sem("s_cu")
    gfA = P.sb("gfA", [MYH, T], F32, ph2)
    gfB = P.sb("gfB", [MYH, T], F32, ph2)
    gfull = P.sb("gfull", [MYH, T], F32, ph2)
    fstage = P.sb("fstage", [MYH, 4096], F32, ph2)
    cupdn = P.sb("cupdn", [MYH, 2], F32, ph2)
    pgt = [P.ps(f"pgt{i}", [MYH, 128], F32, ph2) for i in range(2)]
    n = 0
    for sg in range(64):
        for ty in range(2):
            tpar = n % 2
            w = ([(s_gt, 2 * n - 1)] if n >= 2 else [(env["s_cst"], env["cst_target"])])
            w.append((s_sig, 2 * sg + 1 + ty))
            P.emit("tensor", lambda e, ps=pgt[tpar], t=sgm, sg=sg, ty=ty, idf=env["idn_f"]:
                   e.transpose(ps[:, :], t[:, sg * 8 + ty * 4:sg * 8 + ty * 4 + 4], idf[:, :]),
                   waits=w, incs=[(s_gt, 1)])
            P.emit("vector", lambda e, ps=pgt[tpar], d=(gfA if ty == 0 else gfB), sg=sg:
                   e.tensor_copy(out=d[0:MYH, sg * 128:(sg + 1) * 128], in_=ps[0:MYH, :]),
                   waits=[(s_gt, 2 * n + 1)], incs=[(s_gt, 1)])
            n += 1
    gt_t = P.c(s_gt)
    P.emit("vector", lambda e, b=gfB, g=gccol:
           e.tensor_scalar(out=b[0:MYH, :], in0=b[0:MYH, :], scalar1=g[0:MYH, 0:1],
                           scalar2=None, op0=OP.mult),
           waits=[(s_gt, gt_t), (s_wld, wld_t)])
    P.emit("vector", lambda e, b=gfB:
           e.tensor_scalar(out=b[0:MYH, :], in0=b[0:MYH, :], scalar1=-1.0,
                           scalar2=None, op0=OP.add))
    P.emit("vector", lambda e, a=gfA, b=gfB, f=gfull:
           e.tensor_tensor(out=f[0:MYH, :], in0=a[0:MYH, :], in1=b[0:MYH, :], op=OP.mult))
    P.emit("vector", lambda e, f=gfull:
           e.tensor_scalar(out=f[0:MYH, :], in0=f[0:MYH, :], scalar1=2.0,
                           scalar2=None, op0=OP.add),
           incs=[(s_gout, 1)])
    gout_t = P.c(s_gout)
    P.emit("gpsimd", lambda e, d=env["TAB"], t=cupdn:
           e.dma_start(out=t[:, 0:1], in_=d[:, 9500:9501], allow_slow_non_contiguous=True),
           incs=[(s_cu, 16)])
    P.emit("gpsimd", lambda e, d=env["TAB"], t=cupdn:
           e.dma_start(out=t[:, 1:2], in_=d[:, 7000:7001], allow_slow_non_contiguous=True),
           incs=[(s_cu, 16)])
    P.emit("gpsimd", lambda e, d=GATE, t=gfull: e.dma_start(out=d[:, :], in_=t[0:MYH, :]),
           waits=[(s_gout, gout_t)], incs=[(s_gd, 16)])
    for side, dram, col in [("up", FUP, 0), ("dn", FDN, 1)]:
        for hh in range(2):
            P.emit("scalar", lambda e, gf=gfull, f=fstage, c=cupdn, hh=hh, col=col:
                   e.activation(out=f[0:MYH, :], in_=gf[0:MYH, hh * 4096:(hh + 1) * 4096],
                                func=AF.Exp, scale=c[0:MYH, col:col + 1]),
                   waits=[(s_gout, gout_t), (s_cu, 32), (s_gd, P.c(s_gd))], incs=[(s_gt, 1)])
            P.emit("gpsimd", lambda e, d=dram, f=fstage, hh=hh:
                   e.dma_start(out=d[:, hh * 4096:(hh + 1) * 4096], in_=f[0:MYH, :]),
                   waits=[(s_gt, P.c(s_gt))], incs=[(s_gd, 16)])
    gd_t = P.c(s_gd)
    if debug:
        og = nc.dram_tensor("o_gate", [MYH, T], F32, kind="ExternalOutput")
        P.emit("gpsimd", lambda e, d=og, t=gfull: e.dma_start(out=d[:, :], in_=t[0:MYH, :]),
               incs=[(s_gd, 16)])
        gd_t = P.c(s_gd)
    ph2.close()
    if phases <= 2:
        P.emit("gpsimd", lambda e: None,
               waits=[(s_pe, pe_t), (s_gd, gd_t)] + [(s, v) for s, v in zip(s_pd4, pd_t)])
        P.finalize()
        return nc, P

    ph4 = contextlib.ExitStack()
    s_qk = P.sem("s_qk"); s_band = P.sem("s_band"); s_exp = P.sem("s_exp")
    s_pm = P.sem("s_pm"); s_pv = P.sem("s_pv"); s_ev = P.sem("s_ev")
    s_pf2 = [P.sem("s_pf0"), P.sem("s_pf1")]
    s_rn = P.sem("s_rn"); s_rb = P.sem("s_rb"); s_ctx = P.sem("s_ctx")
    s_hld = P.sem("s_hld"); s_tr = P.sem("s_tr"); s_sp = P.sem("s_sp")
    s_cd2 = [P.sem("s_cd0"), P.sem("s_cd1")]
    strip = P.sb("strip", [128, MYH * 2720], BF16, ph4)
    qTh = P.sb("qTh", [128, T], BF16, ph4)
    kTh = P.sb("kTh", [128, T], BF16, ph4)
    vls = P.sb("vls", [128, T], BF16, ph4)
    Vph = P.sb("Vph", [128, T], BF16, ph4)
    S2 = [P.ps(f"S2_{i}", [128, 1024], F32, ph4) for i in range(2)]
    acc_m = P.ps("acc_m", [128, 512], F32, ph4)
    acc_f = P.ps("acc_f", [128, 512], F32, ph4)
    sums = P.ps("sums", [128, 512], F32, ph4)
    ptr = P.ps("ptr", [128, 128], BF16, ph4)
    expS = [P.sb(f"expS{i}", [128, 1024], BF16, ph4) for i in range(2)]
    tmpb = P.sb("tmpb", [128, 1024], BF16, ph4)
    gbc = [P.sb(f"gbc{i}", [128, 1024], BF16, ph4) for i in range(2)]
    fmbc = [P.sb(f"fmbc{i}", [128, 1024], BF16, ph4) for i in range(2)]
    fjbc = [P.sb("fjbc", [128, 512], F32, ph4)] * 2
    fmrA = [P.sb("fmrA", [1, 512], F32, ph4)] * 2
    fmrB = [P.sb("fmrB", [1, 512], F32, ph4)] * 2
    stg = [P.sb(f"stg{i}", [128, 1024], F32, ph4) for i in range(2)]
    rsA = P.sb("rsA", [1, 512], F32, ph4)
    rsB = P.sb("rsB", [1, 512], F32, ph4)
    rcA = P.sb("rcA", [1, 512], F32, ph4)
    rcB = P.sb("rcB", [1, 512], F32, ph4)
    rcbc = [P.sb(f"rcbc{i}", [128, 512], F32, ph4) for i in range(2)]
    ctmp = [P.sb(f"ctmp{i}", [128, 512], BF16, ph4) for i in range(2)]
    zb = P.sb("zb", [128, 64], BF16, ph4)
    s_zb = P.sem("s_zb")
    P.emit("vector", lambda e, t=zb: e.memset(t[:, :], 0.0), incs=[(s_zb, 1)])

    phase_done = [(s_pe, pe_t), (s_gout, gout_t), (s_gd, gd_t), (s_gt, P.c(s_gt)),
                  (s_sig, sig_t)] + [(s, v) for s, v in zip(s_pd4, pd_t)]
    for h in range(MYH):
        P.emit("sync", lambda e, d=env["TAB"], t=strip, h=h:
               e.dma_start(out=t[:, h * 2720:h * 2720 + WSTRIP],
                           in_=AP(d, h * 16384 + MLO, [[1, 128], [1, WSTRIP]])),
               waits=(phase_done if h == 0 else []), incs=[(s_sp, 16)])
    sp_t = P.c(s_sp)

    qh_info = []
    for qh in range(NQH):
        regs = [region_of(qh, kt) for kt in range(NKT)]
        nup = sum(1 for r, _ in regs if r == "up")
        ndn = sum(1 for r, _ in regs if r == "dn")
        maj = "up" if nup >= ndn else "dn"
        qh_info.append((regs, maj))

    seq = 0
    trn = 0
    for hp in range(2):
        hA, hB = 2 * hp, 2 * hp + 1
        whl = [(s, v) for s, v in zip(s_pd4, pd_t)]
        if hp == 0:
            whl += [(s_gd, gd_t), (s_gt, P.c(s_gt)), (s_gout, gout_t)]
        if hp > 0:
            whl.append((s_qk, seq))
            whl.append((s_pv, seq))
            whl.append((s_ctx, P.c(s_ctx)))
        P.emit("sync", lambda e, d=QTD, t=qTh, hp=hp:
               e.dma_start(out=t[:, :], in_=d[:, hp * T:(hp + 1) * T]),
               waits=whl, incs=[(s_hld, 16)])
        P.emit("sync", lambda e, d=KTD, t=kTh, hp=hp:
               e.dma_start(out=t[:, :], in_=d[:, hp * T:(hp + 1) * T]),
               incs=[(s_hld, 16)])
        P.emit("sync", lambda e, d=VTD, t=vls, hp=hp:
               e.dma_start(out=t[:, :], in_=d[:, hp * T:(hp + 1) * T]),
               incs=[(s_hld, 16)])
        hld_t = P.c(s_hld)
        pv_prev = seq
        for kt in range(NKT):
            w = [(s_hld, hld_t)] + ([(s_tr, 2 * trn)] if trn >= 1 else [])
            P.emit("tensor", lambda e, ps=ptr, v=vls, kt=kt, idn=env["idn"]:
                   e.transpose(ps[:, :], v[:, kt * 128:(kt + 1) * 128], idn[:, :]),
                   waits=w, incs=[(s_tr, 1)])
            wc = [(s_tr, 2 * trn + 1)]
            if kt == 0 and hp > 0:
                wc.append((s_pv, pv_prev))
            P.emit("vector", lambda e, ps=ptr, d=Vph, kt=kt:
                   e.tensor_copy(out=d[:, kt * 128:(kt + 1) * 128], in_=ps[:, :]),
                   waits=wc, incs=[(s_tr, 1)])
            trn += 1
        tr_t = P.c(s_tr)
        for qh in range(NQH):
            regs, maj = qh_info[qh]
            has_band = any(r == "band" for r, _ in regs)
            has_min = any(r not in ("band", maj) for r, _ in regs)
            has_maj = any(r == maj for r, _ in regs)
            qp = qh % 2
            pfw = [(s_gd, gd_t)]
            if seq >= 2 * NKT:
                pfw.append((s_ctx, P.c(s_ctx)))
            mindram = FDN if maj == "up" else FUP
            majdram = FUP if maj == "up" else FDN
            if has_band:
                for i, hh in enumerate((hA, hB)):
                    P.emit("gpsimd", lambda e, d=GATE, t=gbc[qp], hh=hh, i=i, qh=qh:
                           e.dma_start(out=t[:, i * 512:(i + 1) * 512],
                                       in_=d[hh:hh + 1, qh * 512:(qh + 1) * 512].partition_broadcast(128)),
                           waits=(pfw if i == 0 else []), incs=[(s_pf2[qp], 16)])
                    pfw = []
            if has_min:
                for i, hh in enumerate((hA, hB)):
                    P.emit("gpsimd", lambda e, d=mindram, t=fmbc[qp], hh=hh, i=i, qh=qh:
                           e.dma_start(out=t[:, i * 512:(i + 1) * 512],
                                       in_=d[hh:hh + 1, qh * 512:(qh + 1) * 512].partition_broadcast(128)),
                           waits=(pfw if i == 0 else []), incs=[(s_pf2[qp], 16)])
                    pfw = []
            if has_maj:
                for i, hh in enumerate((hA, hB)):
                    P.emit("gpsimd", lambda e, d=majdram, t=fjbc[qp], hh=hh, i=i, qh=qh:
                           e.dma_start(out=t[i * 64:(i + 1) * 64, :],
                                       in_=d[hh:hh + 1, qh * 512:(qh + 1) * 512].partition_broadcast(64)),
                           waits=(pfw if i == 0 else []), incs=[(s_pf2[qp], 16)])
                    pfw = []
                    P.emit("gpsimd", lambda e, d=majdram, t=(fmrA if i == 0 else fmrB)[qp], hh=hh, qh=qh:
                           e.dma_start(out=t[0:1, :], in_=d[hh:hh + 1, qh * 512:(qh + 1) * 512]),
                           incs=[(s_pf2[qp], 16)])
            pf_target = P.c(s_pf2[qp])
            ev_before = P.c(s_ev)
            rn_before = P.c(s_rn)
            rbexp_before = getattr(P, "_rb_after_export", 0)
            wdm = [(s_ev, ev_before), (s_rn, rn_before), (s_zb, 1)]
            accs_used = [acc_m] + ([acc_f] if has_maj else [])
            for a in accs_used:
                for colpos in (0, 64):
                    P.emit("tensor", lambda e, a=a, z=zb, q=qTh, colpos=colpos, qh=qh:
                           e.matmul(a[colpos:colpos + 64, :], z[:, :],
                                    q[0:128, qh * 512:(qh + 1) * 512],
                                    start=True, stop=False, tile_position=(0, colpos),
                                    skip_group_check=True),
                           waits=wdm)
                    wdm = []
            rows_used = [0, 64] + ([32, 96] if has_maj else [])
            for r in rows_used:
                P.emit("tensor", lambda e, s=sums, o=env["ones"], q=qTh, r=r, qh=qh:
                       e.matmul(s[r:r + 1, :], o[:, 2:3], q[0:128, qh * 512:(qh + 1) * 512],
                                start=True, stop=False, tile_position=(0, r),
                                skip_group_check=True),
                       waits=wdm)
                wdm = []
            last_main = max(i for i, (r, _) in enumerate(regs) if r == "band" or r != maj)
            last_far = max((i for i, (r, _) in enumerate(regs) if r == maj), default=-1)
            pend_pv = None
            for kt in range(NKT):
                reg, E = regs[kt]
                p = seq % 2
                wq = []
                if seq >= 2:
                    wq.append((s_exp, seq - 1))
                if seq == 0:
                    wq += [(s_sp, sp_t)]
                if kt == 0:
                    wq += [(s_tr, tr_t)]
                P.emit("tensor", lambda e, ps=S2[p], k=kTh, q=qTh, kt=kt, qh=qh:
                       e.matmul(ps[:, 0:512], k[0:64, kt * 128:(kt + 1) * 128],
                                q[0:64, qh * 512:(qh + 1) * 512],
                                start=True, stop=True, tile_position=(0, 0)),
                       waits=wq)
                P.emit("tensor", lambda e, ps=S2[p], k=kTh, q=qTh, kt=kt, qh=qh:
                       e.matmul(ps[:, 512:1024], k[64:128, kt * 128:(kt + 1) * 128],
                                q[64:128, qh * 512:(qh + 1) * 512],
                                start=True, stop=True, tile_position=(64, 0)),
                       incs=[(s_qk, 1)])
                if reg == "band":
                    for i, hh in enumerate((hA, hB)):
                        P.emit("vector", lambda e, t=tmpb, st=strip, g=gbc[qp], hh=hh, i=i, E=E:
                               e.tensor_tensor(out=t[:, i * 512:(i + 1) * 512],
                                               in0=AP(st, hh * 2720 + E, [[MYH * 2720, 128], [-1, 512]]),
                                               in1=g[:, i * 512:(i + 1) * 512], op=OP.mult),
                               waits=([(s_qk, seq + 1), (s_pf2[qp], pf_target)] if i == 0
                                      else [(s_qk, seq + 1)]))
                    for i in range(2):
                        P.emit("vector", lambda e, ps=S2[p], t=tmpb, i=i:
                               e.tensor_tensor(out=ps[:, i * 512:(i + 1) * 512],
                                               in0=ps[:, i * 512:(i + 1) * 512],
                                               in1=t[:, i * 512:(i + 1) * 512], op=OP.add),
                               incs=[(s_band, 1)])
                we = [(s_qk, seq + 1), (s_band, P.c(s_band))]
                if seq >= 2:
                    we.append((s_pv, seq - 1))
                P.emit("scalar", lambda e, ps=S2[p], d=expS[p], ns=cc(env, "nshift"):
                       e.activation(out=d[:, :], in_=ps[:, :], func=AF.Exp, bias=ns),
                       waits=we, incs=[(s_exp, 1)])
                if reg not in ("band", maj):
                    P.emit("vector", lambda e, d=expS[p], f=fmbc[qp]:
                           e.tensor_tensor(out=d[:, :], in0=d[:, :], in1=f[:, :], op=OP.mult),
                           waits=[(s_exp, seq + 1), (s_pf2[qp], pf_target)], incs=[(s_pm, 1)])
                isf = (reg == maj)
                acc = acc_f if isf else acc_m
                islast = (kt == (last_far if isf else last_main))
                if pend_pv is not None:
                    pend_pv()

                def mk_pv(acc=acc, isf=isf, islast=islast, p=p, kt=kt, seqv=seq,
                          pmv=P.c(s_pm)):
                    wpv = [(s_exp, seqv + 1), (s_pm, pmv)]
                    P.emit("tensor", lambda e, a=acc, v=Vph, d=expS[p], kt=kt, il=islast:
                           e.matmul(a[0:64, :], v[:, kt * 128:kt * 128 + 64],
                                    d[:, 0:512], start=False, stop=il, tile_position=(0, 0),
                                    skip_group_check=True),
                           waits=wpv)
                    P.emit("tensor", lambda e, a=acc, v=Vph, d=expS[p], kt=kt, il=islast:
                           e.matmul(a[64:128, :], v[:, kt * 128 + 64:(kt + 1) * 128],
                                    d[:, 512:1024], start=False, stop=il, tile_position=(0, 64),
                                    skip_group_check=True))
                    rowA = 32 if isf else 0
                    rowB = 96 if isf else 64
                    P.emit("tensor", lambda e, s=sums, o=env["ones"], d=expS[p], rowA=rowA, il=islast:
                           e.matmul(s[rowA:rowA + 1, :], o[:, 0:1], d[:, 0:512],
                                    start=False, stop=il, tile_position=(0, rowA),
                                    skip_group_check=True))
                    P.emit("tensor", lambda e, s=sums, o=env["ones"], d=expS[p], rowB=rowB, il=islast:
                           e.matmul(s[rowB:rowB + 1, :], o[:, 1:2], d[:, 512:1024],
                                    start=False, stop=il, tile_position=(0, rowB),
                                    skip_group_check=True),
                           incs=[(s_pv, 1)])
                pend_pv = mk_pv
                seq += 1
            pend_pv()
            pend_pv = None
            P.emit("vector", lambda e, t=stg[qp], a=acc_m:
                   e.tensor_copy(out=t[:, 0:512], in_=a[:, :]),
                   waits=[(s_pv, seq)], incs=[(s_ev, 1)])
            if has_maj:
                P.emit("vector", lambda e, t=stg[qp], a=acc_f:
                       e.tensor_copy(out=t[:, 512:1024], in_=a[:, :]), incs=[(s_ev, 1)])
            if has_maj:
                P.emit("vector", lambda e, r=rsA, s=sums, f=fmrA[qp]:
                       e.tensor_tensor(out=r[0:1, :], in0=s[32:33, :], in1=f[0:1, :], op=OP.mult),
                       waits=[(s_pv, seq), (s_pf2[qp], pf_target), (s_rb, rbexp_before)])
                P.emit("vector", lambda e, r=rsB, s=sums, f=fmrB[qp]:
                       e.tensor_tensor(out=r[0:1, :], in0=s[96:97, :], in1=f[0:1, :], op=OP.mult))
                P.emit("vector", lambda e, r=rsA, s=sums:
                       e.tensor_tensor(out=r[0:1, :], in0=r[0:1, :], in1=s[0:1, :], op=OP.add))
                P.emit("vector", lambda e, r=rsB, s=sums:
                       e.tensor_tensor(out=r[0:1, :], in0=r[0:1, :], in1=s[64:65, :], op=OP.add),
                       incs=[(s_rn, 1)])
            else:
                P.emit("vector", lambda e, r=rsA, s=sums:
                       e.tensor_copy(out=r[0:1, :], in_=s[0:1, :]),
                       waits=[(s_pv, seq), (s_rb, rbexp_before)])
                P.emit("vector", lambda e, r=rsB, s=sums:
                       e.tensor_copy(out=r[0:1, :], in_=s[64:65, :]), incs=[(s_rn, 1)])
            rn_t = P.c(s_rn)
            P.emit("scalar", lambda e, a=rsA, b=rcA:
                   e.activation(out=b[0:1, :], in_=a[0:1, :], func=AF.Ln),
                   waits=[(s_rn, rn_t)])
            P.emit("scalar", lambda e, a=rsB, b=rcB:
                   e.activation(out=b[0:1, :], in_=a[0:1, :], func=AF.Ln))
            P.emit("scalar", lambda e, a=rcA, b=rsA:
                   e.activation(out=b[0:1, :], in_=a[0:1, :], func=AF.Exp, scale=-1.0))
            P.emit("scalar", lambda e, a=rcB, b=rsB:
                   e.activation(out=b[0:1, :], in_=a[0:1, :], func=AF.Exp, scale=-1.0),
                   incs=[(s_rn, 1)])
            rn_t = P.c(s_rn)
            P.emit("gpsimd", lambda e, d=RB, a=rsA, qp=qp:
                   e.dma_start(out=d[2 * qp:2 * qp + 1, :], in_=a[0:1, :]),
                   waits=[(s_rn, rn_t)], incs=[(s_rb, 16)])
            P.emit("gpsimd", lambda e, d=RB, a=rsB, qp=qp:
                   e.dma_start(out=d[2 * qp + 1:2 * qp + 2, :], in_=a[0:1, :]),
                   incs=[(s_rb, 16)])
            P._rb_after_export = P.c(s_rb)
            rb_t = P.c(s_rb)
            for i in range(2):
                P.emit("gpsimd", lambda e, d=RB, t=rcbc[qp], i=i, qp=qp:
                       e.dma_start(out=t[i * 64:(i + 1) * 64, :],
                                   in_=d[2 * qp + i:2 * qp + i + 1, :].partition_broadcast(64)),
                       waits=([(s_rb, rb_t)] if i == 0 else []), incs=[(s_rb, 16)])
            rb_t = P.c(s_rb)
            if has_maj:
                P.emit("vector", lambda e, t=stg[qp], f=fjbc[qp]:
                       e.tensor_tensor(out=t[:, 512:1024], in0=t[:, 512:1024], in1=f[:, :], op=OP.mult))
                P.emit("vector", lambda e, t=stg[qp]:
                       e.tensor_tensor(out=t[:, 0:512], in0=t[:, 0:512], in1=t[:, 512:1024], op=OP.add))
            P.emit("vector", lambda e, t=stg[qp], r=rcbc[qp], c=ctmp[qp]:
                   e.tensor_tensor(out=c[:, :], in0=t[:, 0:512], in1=r[:, :], op=OP.mult),
                   waits=[(s_rb, rb_t)] + ([(s_cd2[qp], P.c(s_cd2[qp]))]
                                           if P.c(s_cd2[qp]) else []),
                   incs=[(s_ctx, 1)])
            P.emit("gpsimd", lambda e, d=CTXD, c=ctmp[qp], hp=hp, qh=qh:
                   e.dma_start(out=d[:, hp * T + qh * 512:hp * T + (qh + 1) * 512], in_=c[:, :]),
                   waits=[(s_ctx, P.c(s_ctx))], incs=[(s_cd2[qp], 16)])
    ctx_t = P.c(s_ctx)
    cd_t = [P.c(s) for s in s_cd2]
    ph4.close()
    if phases <= 3:
        P.emit("gpsimd", lambda e: None,
               waits=[(s_ctx, ctx_t)] + [(s, v) for s, v in zip(s_cd2, cd_t)])
        P.finalize()
        return nc, P

    ph5 = contextlib.ExitStack()
    s_op = P.sem("s_op"); s_oe = P.sem("s_oe")
    s_od2 = [P.sem("s_od0"), P.sem("s_od1")]
    s_ol2 = [P.sem("s_ol0"), P.sem("s_ol1")]
    s_ow = P.sem("s_ow")
    ow = P.sb("ow", [128, 2048], BF16, ph5)
    owst = P.sb("owst", [128, 2048], F32, ph5)
    opp = [P.ps(f"opp{i}", [128, 512], F32, ph5) for i in range(2)]
    opst = [P.sb(f"opst{i}", [128, 1024], F32, ph5) for i in range(2)]
    cstg = [P.sb(f"cstg{i}", [128, 256], BF16, ph5) for i in range(2)]
    P.emit("sync", lambda e, d=env["OWT"], t=owst:
           e.dma_start(out=t[:, :], in_=d.ap().rearrange("(c p) e -> p c e", p=128)),
           waits=[(s_ctx, ctx_t)], incs=[(s_ow, 16)])
    P.emit("vector", lambda e, a=owst, b=ow: e.tensor_copy(out=b[:, :], in_=a[:, :]),
           waits=[(s_ow, 16)], incs=[(s_ow, 1)])
    ow_t = P.c(s_ow)
    nop = 0
    for ttg in range(NKT):
        tp = ttg % 2
        wl = [(s, v) for s, v in zip(s_cd2, cd_t)] if ttg == 0 else []
        if ttg >= 2:
            wl = [(s_op, 2 * (ttg - 1))]
        for hp in range(2):
            P.emit("sync", lambda e, d=CTXD, t=cstg[tp], hp=hp, ttg=ttg:
                   e.dma_start(out=t[:, hp * 128:(hp + 1) * 128],
                               in_=d[:, hp * T + ttg * 128:hp * T + (ttg + 1) * 128]),
                   waits=(wl if hp == 0 else []), incs=[(s_ol2[tp], 16)])
        for nchk in range(2):
            for hp in range(2):
                w = []
                if hp == 0:
                    w = [(s_ol2[tp], 32 * (ttg // 2 + 1)), (s_ow, ow_t)]
                    if nop >= 2:
                        w.append((s_oe, nop - 1))
                P.emit("tensor", lambda e, ps=opp[nchk], c=cstg[tp], o=ow, hp=hp, nchk=nchk:
                       e.matmul(ps[:, :], c[:, hp * 128:(hp + 1) * 128],
                                o[:, hp * 1024 + nchk * 512:hp * 1024 + (nchk + 1) * 512],
                                start=(hp == 0), stop=(hp == 1)),
                       waits=w, incs=([(s_op, 1)] if hp == 1 else []))
            nop += 1
            P.emit("vector", lambda e, ps=opp[nchk], t=opst[tp], nchk=nchk:
                   e.tensor_copy(out=t[:, nchk * 512:(nchk + 1) * 512], in_=ps[:, :]),
                   waits=[(s_op, nop)] + ([(s_od2[tp], 16 * (ttg // 2))] if ttg >= 2 else []),
                   incs=[(s_oe, 1)])
        P.emit("gpsimd", lambda e, d=OPART, t=opst[tp], ttg=ttg:
               e.dma_start(out=d[ttg * 128:(ttg + 1) * 128, :], in_=t[:, :]),
               waits=[(s_oe, 2 * (ttg + 1))], incs=[(s_od2[tp], 16)])
    ph5.close()

    P.emit("gpsimd", lambda e: None,
           waits=[(s_od2[0], P.c(s_od2[0])), (s_od2[1], P.c(s_od2[1]))])
    P.finalize()
    return nc, P



def emit_accum_head(P, env, nc, Xd, parts, bias_d, out_sb, pre):
    s_x = P.sem(pre + "_x")
    s_ox = [P.sem(pre + "_ox0"), P.sem(pre + "_ox1")]
    s_ac = P.sem(pre + "_ac")
    st = contextlib.ExitStack()
    obuf = [[P.sb(pre + f"_ob{j}_{i}", [128, 1024], F32, st) for i in range(4)] for j in range(2)]
    bbc = P.sb(pre + "_bbc", [128, 1024], F32, st)
    P.emit("sync", lambda e, d=Xd, t=out_sb:
           e.dma_start(out=t[:, :], in_=d.ap().rearrange("(c p) d -> p c d", p=128)),
           incs=[(s_x, 16)])
    P.emit("sync", lambda e, d=bias_d, t=bbc:
           e.dma_start(out=t[:, :], in_=d.ap().unsqueeze(0).partition_broadcast(128)),
           incs=[(s_x, 16)])
    npart = len(parts)
    for c in range(NTC):
        cp = c % 2
        for i, pd in enumerate(parts):
            w = []
            if i == 0 and c >= 2:
                w = [(s_ac, (npart + 1) * (c - 1))]
            P.emit("sync", lambda e, d=pd, t=obuf[cp][i], c=c:
                   e.dma_start(out=t[:, :], in_=d[c * 128:(c + 1) * 128, :]),
                   waits=w, incs=[(s_ox[cp], 16)])
        sl = slice(c * 1024, (c + 1) * 1024)
        P.emit("vector", lambda e, t=out_sb, o=obuf[cp][0], sl=sl:
               e.tensor_tensor(out=t[:, sl], in0=t[:, sl], in1=o[:, :], op=OP.add),
               waits=[(s_x, 32), (s_ox[cp], 16 * npart * (c // 2 + 1))], incs=[(s_ac, 1)])
        for i in range(1, npart):
            P.emit("vector", lambda e, t=out_sb, o=obuf[cp][i], sl=sl:
                   e.tensor_tensor(out=t[:, sl], in0=t[:, sl], in1=o[:, :], op=OP.add),
                   incs=[(s_ac, 1)])
        P.emit("vector", lambda e, t=out_sb, b=bbc, sl=sl:
               e.tensor_tensor(out=t[:, sl], in0=t[:, sl], in1=b[:, :], op=OP.add),
               incs=[(s_ac, 1)])
    st.close()
    return s_ac, P.c(s_ac)


def emit_wload(P, env, dram, dst_bf, stg, s_wl, s_wc):
    base_wl = P.c(s_wl)
    base_wc = P.c(s_wc)
    for half in range(2):
        P.emit("sync", lambda e, d=dram, t=stg, half=half:
               e.dma_start(out=t[:, :], in_=d[half * 512:(half + 1) * 512, :]
                           .rearrange("(c p) n -> p c n", p=128)),
               waits=([(s_wc, base_wc + 1)] if half else []), incs=[(s_wl, 16)])
        P.emit("vector", lambda e, a=stg, b=dst_bf, half=half:
               e.tensor_copy(out=b[:, half * 4096:(half + 1) * 4096], in_=a[:, :]),
               waits=[(s_wl, base_wl + 16 * (half + 1))], incs=[(s_wc, 1)])


def build_stageC(debug=False):
    nc = bass.Bass("TRN2", target_bir_lowering=False, detect_race_conditions=_DETECT)
    env = {}
    ti = lambda n, shp, dt=F32: nc.dram_tensor(n, shp, dt, kind="ExternalInput")
    env["X"] = ti("X", [TL, D])
    OPs = [ti(f"OP{i}", [TL, D]) for i in range(4)]
    env["OUTB"] = ti("OUTB", [D])
    env["CBT"] = ti("CBT", [D, D])
    env["SCB"] = ti("SCB", [D])
    env["FBIAS"] = ti("FBIAS", [D])
    env["CONSTS"] = ti("CONSTS", [128, 8])
    env["IDN"] = ti("IDN", [128, 128])
    env["ONES"] = ti("ONES", [128, 4])
    X3 = nc.dram_tensor("X3", [TL, D], F32, kind="ExternalOutput")
    LNT3 = nc.dram_tensor("LNT3", [D, TL], BF16, kind="ExternalOutput")

    P = Prog(nc)
    load_consts(P, env)
    x2 = P.sb("x2", [128, NTC * 1024], F32)
    s_ac, ac_t = emit_accum_head(P, env, nc, env["X"], OPs, env["OUTB"], x2, "c")

    ph = contextlib.ExitStack()
    ln2bf = P.sb("ln2bf", [128, NTC * 1024], BF16, ph)
    ln2T = P.sb("ln2T", [128, NDM * TL], BF16, ph)
    P.emit("vector", lambda e: None, waits=[(s_ac, ac_t), (env["s_cst"], env["cst_target"])])
    P.emit("scalar", lambda e: None, waits=[(s_ac, ac_t), (env["s_cst"], env["cst_target"])])
    sl2, sl2_t = emit_ln(P, "ln2", x2, NTC, ln2bf, stack=ph, eps_ap=cc(env, "eps"))
    P.emit("tensor", lambda e: None, waits=[(sl2, sl2_t), (env["s_cst"], env["cst_target"])])
    st2, st2_t = emit_transpose_16(P, "ln2", ln2bf, ln2T, env["idn"], stack=ph)

    s_wl = P.sem("s_wl"); s_wc = P.sem("s_wc")
    wstg = P.sb("wstg", [128, 4 * 1024], F32, ph)
    cbt = P.sb("cbt", [128, NDM * 1024], BF16, ph)
    scb_bc = P.sb("scb_bc", [128, 1024], F32, ph)
    fb_bc = P.sb("fb_bc", [128, 1024], F32, ph)
    emit_wload(P, env, env["CBT"], cbt, wstg, s_wl, s_wc)
    P.emit("sync", lambda e, d=env["SCB"], t=scb_bc:
           e.dma_start(out=t[:, :], in_=d.ap().unsqueeze(0).partition_broadcast(128)),
           incs=[(s_wl, 16)])
    P.emit("sync", lambda e, d=env["FBIAS"], t=fb_bc:
           e.dma_start(out=t[:, :], in_=d.ap().unsqueeze(0).partition_broadcast(128)),
           incs=[(s_wl, 16)])
    P.emit("vector", lambda e, t=scb_bc:
           e.tensor_scalar(out=t[:, :], in0=t[:, :], scalar1=1.0 / 127.0, scalar2=None, op0=OP.mult),
           waits=[(s_wl, 64)], incs=[(s_wc, 1)])
    wc_t = P.c(s_wc)

    s_fm = P.sem("s_fm"); s_fe = P.sem("s_fe")
    pf2 = [P.ps(f"pf{i}", [128, 512], F32, ph) for i in range(2)]
    fstg = P.sb("fstg", [128, 512], F32, ph)
    gidx = 0
    for c in range(NTC):
        for nch in range(2):
            for dm in range(NDM):
                w = []
                if dm == 0:
                    w = [(st2, st2_t), (s_wc, wc_t)]
                    if gidx >= 2:
                        w.append((s_fe, 3 * (gidx - 1)))
                P.emit("tensor", lambda e, ps=pf2[gidx % 2], lt=ln2T, cb=cbt, c=c, nch=nch, dm=dm:
                       e.matmul(ps[:, :], lt[:, dm * TL + c * 128:dm * TL + (c + 1) * 128],
                                cb[:, dm * 1024 + nch * 512:dm * 1024 + (nch + 1) * 512],
                                start=(dm == 0), stop=(dm == NDM - 1)),
                       waits=w, incs=([(s_fm, 1)] if dm == NDM - 1 else []))
            P.emit("vector", lambda e, ps=pf2[gidx % 2], t=fstg, s=scb_bc, nch=nch:
                   e.tensor_tensor(out=t[:, :], in0=ps[:, :], in1=s[:, nch * 512:(nch + 1) * 512],
                                   op=OP.mult),
                   waits=[(s_fm, gidx + 1)], incs=[(s_fe, 1)])
            P.emit("vector", lambda e, t=fstg, b=fb_bc, nch=nch:
                   e.tensor_tensor(out=t[:, :], in0=t[:, :], in1=b[:, nch * 512:(nch + 1) * 512],
                                   op=OP.add),
                   incs=[(s_fe, 1)])
            P.emit("vector", lambda e, t=fstg, x=x2, c=c, nch=nch:
                   e.tensor_tensor(out=x[:, c * 1024 + nch * 512:c * 1024 + (nch + 1) * 512],
                                   in0=x[:, c * 1024 + nch * 512:c * 1024 + (nch + 1) * 512],
                                   in1=t[:, :], op=OP.add),
                   incs=[(s_fe, 1)])
            gidx += 1
    fe_t = P.c(s_fe)
    ph.close()

    ph2 = contextlib.ExitStack()
    s_out = P.sem("s_out")
    P.emit("gpsimd", lambda e, d=X3, t=x2:
           e.dma_start(out=d.ap().rearrange("(c p) d -> p c d", p=128),
                       in_=t[:, :].rearrange("p (c d) -> p c d", c=NTC)),
           waits=[(s_fe, fe_t)], incs=[(s_out, 16)])
    ln3bf = P.sb("ln3bf", [128, NTC * 1024], BF16, ph2)
    ln3T = P.sb("ln3T", [128, NDM * TL], BF16, ph2)
    P.emit("vector", lambda e: None, waits=[(s_fe, fe_t)])
    P.emit("scalar", lambda e: None, waits=[(s_fe, fe_t)])
    sl3, sl3_t = emit_ln(P, "ln3", x2, NTC, ln3bf, stack=ph2, eps_ap=cc(env, "eps"))
    P.emit("tensor", lambda e: None, waits=[(sl3, sl3_t)])
    st3, st3_t = emit_transpose_16(P, "ln3", ln3bf, ln3T, env["idn"], stack=ph2)
    P.emit("gpsimd", lambda e, d=LNT3, t=ln3T:
           e.dma_start(out=d.ap().rearrange("(j p) t -> p j t", p=128),
                       in_=t[:, :].rearrange("p (j t) -> p j t", j=NDM)),
           waits=[(st3, st3_t)], incs=[(s_out, 16)])
    P.emit("gpsimd", lambda e: None, waits=[(s_out, P.c(s_out))])
    ph2.close()
    P.finalize()
    return nc, P


def build_stageD(debug=False):
    nc = bass.Bass("TRN2", target_bir_lowering=False, detect_race_conditions=_DETECT)
    env = {}
    ti = lambda n, shp, dt=F32: nc.dram_tensor(n, shp, dt, kind="ExternalInput")
    env["LNT"] = ti("LNT", [D, T], BF16)
    env["W1MY"] = ti("W1MY", [D, 4 * MYE])
    env["B1MY"] = ti("B1MY", [4 * MYE])
    env["CONSTS"] = ti("CONSTS", [128, 8])
    env["IDN"] = ti("IDN", [128, 128])
    env["ONES"] = ti("ONES", [128, 4])
    TTD = nc.dram_tensor("TTD", [128, 2 * T], F32, kind="ExternalOutput")
    STS = nc.dram_tensor("STS", [1, T], F32, kind="ExternalOutput")
    STQ = nc.dram_tensor("STQ", [1, T], F32, kind="ExternalOutput")
    ik = "ExternalOutput" if debug else "Internal"
    QTD = nc.dram_tensor("o_q3", [128, 2 * T], BF16, kind=ik)
    KTD = nc.dram_tensor("o_k3", [128, 2 * T], BF16, kind=ik)
    VTD = nc.dram_tensor("o_v3", [128, 2 * T], BF16, kind=ik)
    UD = nc.dram_tensor("UD", [128, 2 * T], F32)

    P = Prog(nc)
    load_consts(P, env)
    ones_f32 = P.sb("ones_f32", [128, 4], F32)
    s_of = P.sem("s_of")
    P.emit("vector", lambda e, t=ones_f32: e.memset(t[:, 0:2], 1.0), incs=[(s_of, 1)])
    P.emit("vector", lambda e, t=ones_f32: e.memset(t[:, 2:4], 0.0), incs=[(s_of, 1)])

    ph1 = contextlib.ExitStack()
    s_wld = P.sem("s_wld"); s_wcv = P.sem("s_wcv"); s_wst = P.sem("s_wst")
    wstage = P.sb("wstage", [128, 4096], F32, ph1)
    w1 = [P.sb(f"w1_{i}", [128, 4096], BF16, ph1) for i in range(2)]
    b1c = P.sb("b1c", [128, 8], F32, ph1)
    for i in range(2):
        P.emit("sync", lambda e, d=env["W1MY"], t=wstage, i=i:
               e.dma_start(out=t[:, :], in_=d[:, i * 512:(i + 1) * 512]
                           .rearrange("(c p) e -> p c e", p=128)),
               waits=([(s_wcv, i)] if i else []), incs=[(s_wst, 16)])
        P.emit("vector", lambda e, a=wstage, b=w1[i]: e.tensor_copy(out=b[:, :], in_=a[:, :]),
               waits=[(s_wst, 16 * (i + 1))], incs=[(s_wcv, 1)])
    P.emit("sync", lambda e, d=env["B1MY"], t=b1c:
           e.dma_start(out=t[:, :], in_=d.ap().rearrange("(c p) -> p c", p=128),
                       allow_slow_non_contiguous=True),
           incs=[(s_wld, 16)])
    s_lnt2 = [P.sem("s_lnt0"), P.sem("s_lnt1")]
    s_mm = P.sem("s_mm"); s_pe = P.sem("s_pe")
    s_pd4 = [P.sem(f"s_pd{i}") for i in range(4)]
    s_pu2 = [P.sem(f"s_pu{i}") for i in range(2)]
    lnt_sb = [P.sb(f"lnt{i}", [128, NDM * 512], BF16, ph1) for i in range(2)]
    ustg = [P.sb(f"ustg{i}", [128, 512], F32, ph1) for i in range(2)]
    pstg = [P.sb(f"pstg{i}", [128, 512], BF16, ph1) for i in range(4)]
    pp = [P.ps(f"pp{i}", [128, 512], F32, ph1) for i in range(2)]
    NT512 = T // 512
    gidx = 0
    vidx = 0
    uidx = 0
    for tt in range(NT512):
        par = tt % 2
        w = [(s_mm, 8 * (tt - 1))] if tt >= 2 else []
        P.emit("sync", lambda e, d=env["LNT"], t=lnt_sb[par], tt=tt:
               e.dma_start(out=t[:, :], in_=d[:, tt * 512:(tt + 1) * 512]
                           .rearrange("(c p) t -> p c t", p=128)),
               waits=w, incs=[(s_lnt2[par], 16)])
        for sli, dstd in [(0, UD), (1, VTD), (2, QTD), (3, KTD)]:
            for ech in range(2):
                wi = sli // 2
                col = (sli % 2) * 256 + ech * 128
                for c in range(NDM):
                    wt = []
                    if c == 0:
                        wt = [(s_lnt2[par], 16 * (tt // 2 + 1)), (s_wcv, 2), (s_wld, 16)]
                        if gidx >= 2:
                            wt.append((s_pe, gidx - 1))
                    P.emit("tensor", lambda e, ps=pp[gidx % 2], wsb=w1[wi], c=c, col=col,
                           t=lnt_sb[par]:
                           e.matmul(ps[:, :], wsb[:, c * 512 + col:c * 512 + col + 128],
                                    t[:, c * 512:(c + 1) * 512],
                                    start=(c == 0), stop=(c == NDM - 1)),
                           waits=wt, incs=([(s_mm, 1)] if c == NDM - 1 else []))
                bcol = sli * 2 + ech
                wpe = [(s_mm, gidx + 1)]
                if sli == 0:
                    if uidx >= 2:
                        wpe.append((s_pu2[uidx % 2], 16 * (uidx // 2)))
                    P.emit("scalar", lambda e, ps=pp[gidx % 2], st=ustg[uidx % 2], bcol=bcol, b=b1c:
                           e.activation(out=st[:, :], in_=ps[:, :], func=AF.Silu,
                                        bias=b[:, bcol:bcol + 1]),
                           waits=wpe, incs=[(s_pe, 1)])
                    P.emit("sync", lambda e, d=UD, st=ustg[uidx % 2], ech=ech, tt=tt:
                           e.dma_start(out=d[:, ech * T + tt * 512:ech * T + (tt + 1) * 512],
                                       in_=st[:, :]),
                           waits=[(s_pe, gidx + 1)], incs=[(s_pu2[uidx % 2], 16)])
                    uidx += 1
                else:
                    if vidx >= 4:
                        wpe.append((s_pd4[vidx % 4], 16 * (vidx // 4)))
                    P.emit("scalar", lambda e, ps=pp[gidx % 2], st=pstg[vidx % 4], bcol=bcol, b=b1c:
                           e.activation(out=st[:, :], in_=ps[:, :], func=AF.Silu,
                                        bias=b[:, bcol:bcol + 1]),
                           waits=wpe, incs=[(s_pe, 1)])
                    P.emit("gpsimd", lambda e, d=dstd, st=pstg[vidx % 4], ech=ech, tt=tt:
                           e.dma_start(out=d[:, ech * T + tt * 512:ech * T + (tt + 1) * 512],
                                       in_=st[:, :]),
                           waits=[(s_pe, gidx + 1)], incs=[(s_pd4[vidx % 4], 16)])
                    vidx += 1
                gidx += 1
    pe_t = P.c(s_pe)
    pd_t = [P.c(s) for s in s_pd4]
    pu_t = [P.c(s) for s in s_pu2]
    ph1.close()
    if debug == "p1":
        P.emit("gpsimd", lambda e: None,
               waits=[(s_pe, pe_t)] + [(s, v) for s, v in zip(s_pd4, pd_t)]
                     + [(s, v) for s, v in zip(s_pu2, pu_t)])
        P.finalize()
        return nc, P

    ph4 = contextlib.ExitStack()
    s_qk = P.sem("s_qk"); s_sx = P.sem("s_sx"); s_pv = P.sem("s_pv")
    s_ev = P.sem("s_ev"); s_hld = P.sem("s_hld"); s_tr = P.sem("s_tr")
    s_uf = [P.sem("s_uf0"), P.sem("s_uf1")]
    s_st = P.sem("s_st"); s_td = [P.sem("s_td0"), P.sem("s_td1")]
    s_zb = P.sem("s_zb"); s_sev = P.sem("s_sev"); s_fin = P.sem("s_fin")
    qTh = P.sb("qTh", [128, T], BF16, ph4)
    kTh = P.sb("kTh", [128, T], BF16, ph4)
    vls = P.sb("vls", [128, T], BF16, ph4)
    Vph = P.sb("Vph", [128, T], BF16, ph4)
    sS = [P.sb(f"sS{i}", [128, 1024], BF16, ph4) for i in range(2)]
    ubuf = [P.sb(f"ubuf{i}", [128, 512], F32, ph4) for i in range(2)]
    tstg = [P.sb(f"tstg{i}", [128, 512], F32, ph4) for i in range(2)]
    sqst = P.sb("sqst", [128, 512], F32, ph4)
    stats_s = P.sb("stats_s", [1, T], F32, ph4)
    stats_q = P.sb("stats_q", [1, T], F32, ph4)
    zb = P.sb("zbD", [128, 64], BF16, ph4)
    zf = P.sb("zfD", [128, 1], F32, ph4)
    S2 = [P.ps(f"S2_{i}", [128, 1024], F32, ph4) for i in range(2)]
    av = P.ps("av", [128, 512], F32, ph4)
    stp = P.ps("stp", [128, 512], F32, ph4)
    ptr = P.ps("ptr", [128, 128], BF16, ph4)
    P.emit("vector", lambda e, t=zb: e.memset(t[:, :], 0.0), incs=[(s_zb, 1)])
    P.emit("vector", lambda e, t=zf: e.memset(t[:, :], 0.0), incs=[(s_zb, 1)])
    seq = 0
    trn = 0
    for hp in range(2):
        whl = [(s, v) for s, v in zip(s_pd4, pd_t)] + [(s, v) for s, v in zip(s_pu2, pu_t)]
        if hp > 0:
            whl += [(s_qk, seq), (s_pv, seq), (s_ev, P.c(s_ev))]
        P.emit("sync", lambda e, d=QTD, t=qTh, hp=hp:
               e.dma_start(out=t[:, :], in_=d[:, hp * T:(hp + 1) * T]),
               waits=whl, incs=[(s_hld, 16)])
        P.emit("sync", lambda e, d=KTD, t=kTh, hp=hp:
               e.dma_start(out=t[:, :], in_=d[:, hp * T:(hp + 1) * T]),
               incs=[(s_hld, 16)])
        P.emit("sync", lambda e, d=VTD, t=vls, hp=hp:
               e.dma_start(out=t[:, :], in_=d[:, hp * T:(hp + 1) * T]),
               incs=[(s_hld, 16)])
        hld_t = P.c(s_hld)
        pv_prev = seq
        for kt in range(NKT):
            w = [(s_hld, hld_t)] + ([(s_tr, 2 * trn)] if trn >= 1 else [])
            P.emit("tensor", lambda e, ps=ptr, v=vls, kt=kt, idn=env["idn"]:
                   e.transpose(ps[:, :], v[:, kt * 128:(kt + 1) * 128], idn[:, :]),
                   waits=w, incs=[(s_tr, 1)])
            wc = [(s_tr, 2 * trn + 1)]
            if kt == 0 and hp > 0:
                wc.append((s_pv, pv_prev))
            P.emit("vector", lambda e, ps=ptr, d=Vph, kt=kt:
                   e.tensor_copy(out=d[:, kt * 128:(kt + 1) * 128], in_=ps[:, :]),
                   waits=wc, incs=[(s_tr, 1)])
            trn += 1
        tr_t = P.c(s_tr)
        for qh in range(NQH):
            qp = qh % 2
            nq = hp * NQH + qh
            wu = []
            if nq >= 2:
                wu.append((s_ev, P.c(s_ev)))
            P.emit("sync", lambda e, d=UD, t=ubuf[qp], hp=hp, qh=qh:
                   e.dma_start(out=t[:, :], in_=d[:, hp * T + qh * 512:hp * T + (qh + 1) * 512]),
                   waits=wu, incs=[(s_uf[qp], 16)])
            uf_t = P.c(s_uf[qp])
            ev_before = P.c(s_ev)
            P.emit("tensor", lambda e, a=av, z=zb, q=qTh, qh=qh:
                   e.matmul(a[0:64, :], z[:, :], q[0:128, qh * 512:(qh + 1) * 512],
                            start=True, stop=False, tile_position=(0, 0), skip_group_check=True),
                   waits=[(s_ev, ev_before), (s_zb, 2)])
            P.emit("tensor", lambda e, a=av, z=zb, q=qTh, qh=qh:
                   e.matmul(a[64:128, :], z[:, :], q[0:128, qh * 512:(qh + 1) * 512],
                            start=True, stop=False, tile_position=(0, 64), skip_group_check=True))
            pend_pv = None
            for kt in range(NKT):
                p = seq % 2
                wq = []
                if seq >= 2:
                    wq.append((s_sx, seq - 1))
                if kt == 0:
                    wq.append((s_tr, tr_t))
                P.emit("tensor", lambda e, ps=S2[p], k=kTh, q=qTh, kt=kt, qh=qh:
                       e.matmul(ps[:, 0:512], k[0:64, kt * 128:(kt + 1) * 128],
                                q[0:64, qh * 512:(qh + 1) * 512],
                                start=True, stop=True, tile_position=(0, 0)),
                       waits=wq)
                P.emit("tensor", lambda e, ps=S2[p], k=kTh, q=qTh, kt=kt, qh=qh:
                       e.matmul(ps[:, 512:1024], k[64:128, kt * 128:(kt + 1) * 128],
                                q[64:128, qh * 512:(qh + 1) * 512],
                                start=True, stop=True, tile_position=(64, 0)),
                       incs=[(s_qk, 1)])
                we = [(s_qk, seq + 1)]
                if seq >= 2:
                    we.append((s_pv, seq - 1))
                P.emit("scalar", lambda e, ps=S2[p], d=sS[p]:
                       e.activation(out=d[:, :], in_=ps[:, :], func=AF.Silu),
                       waits=we, incs=[(s_sx, 1)])
                if pend_pv is not None:
                    pend_pv()
                il = (kt == NKT - 1)

                def mk_pv(p=p, kt=kt, il=il, seqv=seq):
                    P.emit("tensor", lambda e, a=av, v=Vph, d=sS[p], kt=kt, il=il:
                           e.matmul(a[0:64, :], v[:, kt * 128:kt * 128 + 64],
                                    d[:, 0:512], start=False, stop=il, tile_position=(0, 0),
                                    skip_group_check=True),
                           waits=[(s_sx, seqv + 1)])
                    P.emit("tensor", lambda e, a=av, v=Vph, d=sS[p], kt=kt, il=il:
                           e.matmul(a[64:128, :], v[:, kt * 128 + 64:(kt + 1) * 128],
                                    d[:, 512:1024], start=False, stop=il, tile_position=(0, 64),
                                    skip_group_check=True),
                           incs=[(s_pv, 1)])
                pend_pv = mk_pv
                seq += 1
            pend_pv()
            pend_pv = None
            P.emit("vector", lambda e, t=tstg[qp], a=av, u=ubuf[qp]:
                   e.tensor_tensor(out=t[:, :], in0=a[:, :], in1=u[:, :], op=OP.mult),
                   waits=[(s_pv, seq), (s_uf[qp], uf_t), (s_td[qp], 16 * (nq // 2))],
                   incs=[(s_ev, 1)])
            P.emit("gpsimd", lambda e, d=TTD, t=tstg[qp], hp=hp, qh=qh:
                   e.dma_start(out=d[:, hp * T + qh * 512:hp * T + (qh + 1) * 512], in_=t[:, :]),
                   waits=[(s_ev, P.c(s_ev))], incs=[(s_td[qp], 16)])
            P.emit("scalar", lambda e, t=tstg[qp], q=sqst:
                   e.activation(out=q[:, :], in_=t[:, :], func=AF.Square),
                   waits=[(s_ev, P.c(s_ev)), (s_st, P.c(s_st))], incs=[(s_ev, 1)])
            P.emit("tensor", lambda e, s=stp, z=zf, t=tstg[qp]:
                   e.matmul(s[0:1, :], z[:, :], t[:, :],
                            start=True, stop=False, tile_position=(0, 0), skip_group_check=True),
                   waits=[(s_ev, P.c(s_ev)), (s_sev, P.c(s_sev)), (s_zb, 2)])
            P.emit("tensor", lambda e, s=stp, z=zf, t=tstg[qp]:
                   e.matmul(s[32:33, :], z[:, :], t[:, :],
                            start=True, stop=False, tile_position=(0, 32), skip_group_check=True))
            P.emit("tensor", lambda e, s=stp, o=ones_f32, t=tstg[qp]:
                   e.matmul(s[0:1, :], o[:, 0:1], t[:, :],
                            start=False, stop=False, tile_position=(0, 0), skip_group_check=True),
                   waits=[(s_of, 2)])
            P.emit("tensor", lambda e, s=stp, o=ones_f32, q=sqst:
                   e.matmul(s[32:33, :], o[:, 1:2], q[:, :],
                            start=False, stop=True, tile_position=(0, 32), skip_group_check=True),
                   waits=[(s_ev, P.c(s_ev))], incs=[(s_st, 1)])
            if hp == 0:
                P.emit("vector", lambda e, ss=stats_s, s=stp, qh=qh:
                       e.tensor_copy(out=ss[0:1, qh * 512:(qh + 1) * 512], in_=s[0:1, :]),
                       waits=[(s_st, P.c(s_st))])
                P.emit("vector", lambda e, sq=stats_q, s=stp, qh=qh:
                       e.tensor_copy(out=sq[0:1, qh * 512:(qh + 1) * 512], in_=s[32:33, :]),
                       incs=[(s_sev, 1)])
            else:
                P.emit("vector", lambda e, ss=stats_s, s=stp, qh=qh:
                       e.tensor_tensor(out=ss[0:1, qh * 512:(qh + 1) * 512],
                                       in0=s[0:1, :], in1=ss[0:1, qh * 512:(qh + 1) * 512],
                                       op=OP.add),
                       waits=[(s_st, P.c(s_st))])
                P.emit("vector", lambda e, sq=stats_q, s=stp, qh=qh:
                       e.tensor_tensor(out=sq[0:1, qh * 512:(qh + 1) * 512],
                                       in0=s[32:33, :], in1=sq[0:1, qh * 512:(qh + 1) * 512],
                                       op=OP.add),
                       incs=[(s_sev, 1)])
    P.emit("gpsimd", lambda e, d=STS, t=stats_s: e.dma_start(out=d[:, :], in_=t[0:1, :]),
           waits=[(s_sev, P.c(s_sev))], incs=[(s_fin, 16)])
    P.emit("gpsimd", lambda e, d=STQ, t=stats_q: e.dma_start(out=d[:, :], in_=t[0:1, :]),
           incs=[(s_fin, 16)])
    P.emit("gpsimd", lambda e: None,
           waits=[(s_fin, 32), (s_td[0], P.c(s_td[0])), (s_td[1], P.c(s_td[1]))])
    ph4.close()
    P.finalize()
    return nc, P



def build_stageE(debug=False):
    nc = bass.Bass("TRN2", target_bir_lowering=False, detect_race_conditions=_DETECT)
    env = {}
    ti = lambda n, shp, dt=F32: nc.dram_tensor(n, shp, dt, kind="ExternalInput")
    env["TTD"] = ti("TTD", [128, 2 * T])
    env["SS4"] = ti("SS4", [4, T])
    env["SQ4"] = ti("SQ4", [4, T])
    env["GAM"] = ti("GAM", [MYE])
    env["BET"] = ti("BET", [MYE])
    env["W2MY"] = ti("W2MY", [MYE, D])
    env["CONSTS"] = ti("CONSTS", [128, 8])
    env["IDN"] = ti("IDN", [128, 128])
    env["ONES"] = ti("ONES", [128, 4])
    WPART = nc.dram_tensor("WPART", [T, D], F32, kind="ExternalOutput")
    MURS = nc.dram_tensor("MURS", [2, T], F32)
    if debug:
        o_murs = nc.dram_tensor("o_murs", [2, T], F32, kind="ExternalOutput")

    P = Prog(nc)
    load_consts(P, env)
    s_ld = P.sem("s_ld"); s_rs = P.sem("s_rs"); s_mb = P.sem("s_mb")
    s_zz = P.sem("s_zz"); s_sm = P.sem("s_sm"); s_se = P.sem("s_se"); s_sv = P.sem("s_sv")
    tA = P.sb("tA", [128, T], F32)
    tB = P.sb("tB", [128, T], F32)
    onec = P.sb("onec", [128, 1], F32)
    psA = [P.ps(f"psA{i}", [1, 512], F32) for i in range(2)]
    P.emit("vector", lambda e, t=tA: e.memset(t[:, :], 0.0), incs=[(s_zz, 1)])
    P.emit("vector", lambda e, t=onec: e.memset(t[:, :], 1.0), incs=[(s_zz, 1)])
    for i in range(4):
        P.emit("sync", lambda e, d=env["SS4"], t=tA, i=i:
               e.dma_start(out=t[32 * i:32 * i + 1, :], in_=d[i:i + 1, :]),
               waits=([(s_zz, 2)] if i == 0 else []), incs=[(s_ld, 16)])
    for ch in range(16):
        w = [(s_se, ch - 1)] if ch >= 2 else [(s_ld, 64), (s_zz, 2)]
        P.emit("tensor", lambda e, ps=psA[ch % 2], o=onec, t=tA, ch=ch:
               e.matmul(ps[0:1, :], o[:, 0:1], t[:, ch * 512:(ch + 1) * 512],
                        start=True, stop=True),
               waits=w, incs=[(s_sm, 1)])
        P.emit("scalar", lambda e, ps=psA[ch % 2], b=tB, ch=ch:
               e.activation(out=b[0:1, ch * 512:(ch + 1) * 512], in_=ps[0:1, :],
                            func=AF.Identity, scale=1.0 / 1024.0),
               waits=[(s_sm, ch + 1)], incs=[(s_se, 1)])
    for i in range(4):
        P.emit("sync", lambda e, d=env["SQ4"], t=tA, i=i:
               e.dma_start(out=t[32 * i:32 * i + 1, :], in_=d[i:i + 1, :]),
               waits=([(s_sm, 16)] if i == 0 else []), incs=[(s_ld, 16)])
    P.emit("vector", lambda e, b=tB:
           e.tensor_tensor(out=b[32:33, :], in0=b[0:1, :], in1=b[0:1, :], op=OP.mult),
           waits=[(s_se, 16), (env["s_cst"], env["cst_target"])], incs=[(s_sv, 1)])
    for ch in range(16):
        w = [(s_ld, 128)] if ch == 0 else []
        if ch >= 2:
            w.append((s_sv, ch - 1 + 1))
        P.emit("tensor", lambda e, ps=psA[ch % 2], o=onec, t=tA, ch=ch:
               e.matmul(ps[0:1, :], o[:, 0:1], t[:, ch * 512:(ch + 1) * 512],
                        start=True, stop=True),
               waits=w, incs=[(s_sm, 1)])
        P.emit("vector", lambda e, ps=psA[ch % 2], b=tB, ch=ch:
               e.scalar_tensor_tensor(out=b[64:65, ch * 512:(ch + 1) * 512], in0=ps[0:1, :],
                                      scalar=1.0 / 1024.0, in1=b[32:33, ch * 512:(ch + 1) * 512],
                                      op0=OP.mult, op1=OP.subtract),
               waits=[(s_sm, 16 + ch + 1), (s_sv, 1)], incs=[(s_sv, 1)])
    P.emit("scalar", lambda e, b=tB, ec=env["consts"]:
           e.activation(out=b[32:33, :], in_=b[64:65, :], func=AF.Ln,
                        bias=ec[0:1, CN["eps1"]:CN["eps1"] + 1]),
           waits=[(s_sv, 17)])
    P.emit("scalar", lambda e, b=tB:
           e.activation(out=b[96:97, :], in_=b[32:33, :], func=AF.Exp, scale=-0.5),
           incs=[(s_rs, 1)])
    P.emit("gpsimd", lambda e, d=MURS, b=tB: e.dma_start(out=d[0:1, :], in_=b[0:1, :]),
           waits=[(s_rs, 1)], incs=[(s_mb, 16)])
    P.emit("gpsimd", lambda e, d=MURS, b=tB: e.dma_start(out=d[1:2, :], in_=b[96:97, :]),
           incs=[(s_mb, 16)])
    if debug:
        P.emit("gpsimd", lambda e, d=o_murs, b=tB: e.dma_start(out=d[0:1, :], in_=b[0:1, :]),
               incs=[(s_mb, 16)])
        P.emit("gpsimd", lambda e, d=o_murs, b=tB: e.dma_start(out=d[1:2, :], in_=b[96:97, :]),
               incs=[(s_mb, 16)])
    mbw = P.c(s_mb)

    s_wl = P.sem("s_wl"); s_wc = P.sem("s_wc")
    gamc = P.sb("gamc", [128, 2], F32)
    betc = P.sb("betc", [128, 2], F32)
    w2st = P.sb("w2st", [128, 2048], F32)
    w2 = P.sb("w2", [128, 2048], BF16)
    P.emit("sync", lambda e, d=env["GAM"], t=gamc:
           e.dma_start(out=t[:, :], in_=d.ap().rearrange("(c p) -> p c", p=128),
                       allow_slow_non_contiguous=True), incs=[(s_wl, 16)])
    P.emit("sync", lambda e, d=env["BET"], t=betc:
           e.dma_start(out=t[:, :], in_=d.ap().rearrange("(c p) -> p c", p=128),
                       allow_slow_non_contiguous=True), incs=[(s_wl, 16)])
    P.emit("sync", lambda e, d=env["W2MY"], t=w2st:
           e.dma_start(out=t[:, :], in_=d.ap().rearrange("(c p) e -> p c e", p=128)),
           incs=[(s_wl, 16)])
    P.emit("vector", lambda e, a=w2st, b=w2: e.tensor_copy(out=b[:, :], in_=a[:, :]),
           waits=[(s_wl, 48)], incs=[(s_wc, 1)])

    TH = T // 2
    s_tl = P.sem("s_tl"); s_y = P.sem("s_y")
    mu_bc = P.sb("mu_bc", [128, TH], F32)
    rs_bc = P.sb("rs_bc", [128, TH], F32)
    tTh = P.sb("tTh", [128, TH], F32)
    yT = P.sb("yT", [128, 2 * T], BF16)
    nh = 0
    for half in range(2):
        P.emit("gpsimd", lambda e, d=MURS, t=mu_bc, half=half:
               e.dma_start(out=t[:, :], in_=d[0:1, half * TH:(half + 1) * TH]
                           .partition_broadcast(128)),
               waits=[(s_mb, mbw)] + ([(s_y, P.c(s_y))] if half else []), incs=[(s_mb, 16)])
        P.emit("gpsimd", lambda e, d=MURS, t=rs_bc, half=half:
               e.dma_start(out=t[:, :], in_=d[1:2, half * TH:(half + 1) * TH]
                           .partition_broadcast(128)),
               incs=[(s_mb, 16)])
        mh = P.c(s_mb)
        for hp in range(2):
            P.emit("sync", lambda e, d=env["TTD"], t=tTh, hp=hp, half=half:
                   e.dma_start(out=t[:, :], in_=d[:, hp * T + half * TH:hp * T + (half + 1) * TH]),
                   waits=([(s_y, P.c(s_y))] if nh else []), incs=[(s_tl, 16)])
            P.emit("vector", lambda e, t=tTh, m=mu_bc:
                   e.tensor_tensor(out=t[:, :], in0=t[:, :], in1=m[:, :], op=OP.subtract),
                   waits=[(s_tl, 16 * (nh + 1)), (s_mb, mh)], incs=[(s_y, 1)])
            P.emit("vector", lambda e, t=tTh, r=rs_bc:
                   e.tensor_tensor(out=t[:, :], in0=t[:, :], in1=r[:, :], op=OP.mult),
                   incs=[(s_y, 1)])
            P.emit("vector", lambda e, t=tTh, d=yT, g=gamc, b=betc, hp=hp, half=half:
                   e.tensor_scalar(out=d[:, hp * T + half * TH:hp * T + (half + 1) * TH],
                                   in0=t[:, :], scalar1=g[:, hp:hp + 1], scalar2=b[:, hp:hp + 1],
                                   op0=OP.mult, op1=OP.add),
                   waits=[(s_wl, 48)], incs=[(s_y, 1)])
            nh += 1
    y_t = P.c(s_y)

    s_op = P.sem("s_op"); s_oe = P.sem("s_oe")
    s_od2 = [P.sem("s_od0"), P.sem("s_od1")]
    opp = [P.ps(f"opp{i}", [128, 512], F32) for i in range(2)]
    opst = [P.sb(f"opst{i}", [128, 1024], F32) for i in range(2)]
    nop = 0
    for ttg in range(NKT):
        tp = ttg % 2
        for nchk in range(2):
            for hp in range(2):
                w = []
                if hp == 0:
                    w = [(s_y, y_t), (s_wc, 1)]
                    if nop >= 2:
                        w.append((s_oe, nop - 1))
                P.emit("tensor", lambda e, ps=opp[nchk], c=yT, o=w2, hp=hp, ttg=ttg, nchk=nchk:
                       e.matmul(ps[:, :], c[:, hp * T + ttg * 128:hp * T + (ttg + 1) * 128],
                                o[:, hp * 1024 + nchk * 512:hp * 1024 + (nchk + 1) * 512],
                                start=(hp == 0), stop=(hp == 1)),
                       waits=w, incs=([(s_op, 1)] if hp == 1 else []))
            nop += 1
            P.emit("vector", lambda e, ps=opp[nchk], t=opst[tp], nchk=nchk:
                   e.tensor_copy(out=t[:, nchk * 512:(nchk + 1) * 512], in_=ps[:, :]),
                   waits=[(s_op, nop)] + ([(s_od2[tp], 16 * (ttg // 2))] if ttg >= 2 else []),
                   incs=[(s_oe, 1)])
        P.emit("gpsimd", lambda e, d=WPART, t=opst[tp], ttg=ttg:
               e.dma_start(out=d[ttg * 128:(ttg + 1) * 128, :], in_=t[:, :]),
               waits=[(s_oe, 2 * (ttg + 1))], incs=[(s_od2[tp], 16)])
    P.emit("gpsimd", lambda e: None,
           waits=[(s_od2[0], P.c(s_od2[0])), (s_od2[1], P.c(s_od2[1]))])
    P.finalize()
    return nc, P


def build_stageF(debug=False):
    nc = bass.Bass("TRN2", target_bir_lowering=False, detect_race_conditions=_DETECT)
    env = {}
    ti = lambda n, shp, dt=F32: nc.dram_tensor(n, shp, dt, kind="ExternalInput")
    env["X3"] = ti("X3", [TL, D])
    WPs = [ti(f"WP{i}", [TL, D]) for i in range(4)]
    env["B2"] = ti("B2", [D])
    env["WR"] = ti("WR", [D])
    env["CONSTS"] = ti("CONSTS", [128, 8])
    env["IDN"] = ti("IDN", [128, 128])
    env["ONES"] = ti("ONES", [128, 4])
    X4 = nc.dram_tensor("X4", [TL, D], F32, kind="ExternalOutput")
    SCL = nc.dram_tensor("SCL", [128, NTC], F32, kind="ExternalOutput")

    P = Prog(nc)
    load_consts(P, env)
    x4 = P.sb("x4", [128, NTC * 1024], F32)
    s_ac, ac_t = emit_accum_head(P, env, nc, env["X3"], WPs, env["B2"], x4, "f")
    s_out = P.sem("s_out")
    P.emit("gpsimd", lambda e, d=X4, t=x4:
           e.dma_start(out=d.ap().rearrange("(c p) d -> p c d", p=128),
                       in_=t[:, :].rearrange("p (c d) -> p c d", c=NTC)),
           waits=[(s_ac, ac_t)], incs=[(s_out, 16)])
    ln4f = P.sb("ln4f", [128, NTC * 1024], F32)
    P.emit("vector", lambda e: None, waits=[(s_ac, ac_t), (env["s_cst"], env["cst_target"])])
    P.emit("scalar", lambda e: None, waits=[(s_ac, ac_t), (env["s_cst"], env["cst_target"])])
    sl4, sl4_t = emit_ln(P, "ln4", x4, NTC, None, dst_f32=ln4f, eps_ap=cc(env, "eps"))
    s_sc = P.sem("s_sc")
    wr_bc = P.sb("wr_bc", [128, 1024], F32)
    sct = P.sb("sct", [128, 1024], F32)
    scr = P.sb("scr", [128, NTC], F32)
    P.emit("sync", lambda e, d=env["WR"], t=wr_bc:
           e.dma_start(out=t[:, :], in_=d.ap().unsqueeze(0).partition_broadcast(128)),
           incs=[(s_sc, 16)])
    for c in range(NTC):
        P.emit("vector", lambda e, l=ln4f, w=wr_bc, t=sct, c=c:
               e.tensor_tensor(out=t[:, :], in0=l[:, c * 1024:(c + 1) * 1024], in1=w[:, :],
                               op=OP.mult),
               waits=([(s_sc, 16), (sl4, sl4_t)] if c == 0 else []))
        P.emit("vector", lambda e, t=sct, s=scr, c=c:
               e.reduce_sum(s[:, c:c + 1], t[:, :], axis=mybir.AxisListType.X),
               incs=[(s_sc, 1)])
    P.emit("gpsimd", lambda e, d=SCL, t=scr: e.dma_start(out=d[:, :], in_=t[:, :]),
           waits=[(s_sc, 16 + NTC)], incs=[(s_out, 16)])
    P.emit("gpsimd", lambda e: None, waits=[(s_out, 32)])
    P.finalize()
    return nc, P


def build_stageG(debug=False):
    nc = bass.Bass("TRN2", target_bir_lowering=False, detect_race_conditions=_DETECT)
    env = {}
    ti = lambda n, shp, dt=F32: nc.dram_tensor(n, shp, dt, kind="ExternalInput")
    env["X4"] = ti("X4", [TL, D])
    env["SCALL"] = ti("SCALL", [128, 64])
    env["SCMY"] = ti("SCMY", [128, NTC])
    env["WBLK"] = ti("WBLK", [D, D])
    env["RAMP1"] = ti("RAMP1", [1, 256])
    env["STEPC"] = ti("STEPC", [1, 8])
    env["LO0"] = ti("LO0", [1, 8])
    env["CONSTS"] = ti("CONSTS", [128, 8])
    env["IDN"] = ti("IDN", [128, 128])
    env["ONES"] = ti("ONES", [128, 4])
    OUT = nc.dram_tensor("OUT", [TL, D], F32, kind="ExternalOutput")
    if debug:
        o_thr = nc.dram_tensor("o_thr", [1, 8], F32, kind="ExternalOutput")
        o_msk = nc.dram_tensor("o_msk", [128, NTC], F32, kind="ExternalOutput")

    P = Prog(nc)
    load_consts(P, env)
    s_ld = P.sem("s_ld")
    x4 = P.sb("x4", [128, NTC * 1024], F32)
    P.emit("sync", lambda e, d=env["X4"], t=x4:
           e.dma_start(out=t[:, :], in_=d.ap().rearrange("(c p) d -> p c d", p=128)),
           incs=[(s_ld, 16)])
    ln4bf = P.sb("ln4bf", [128, NTC * 1024], BF16)
    ln4T = P.sb("ln4T", [128, NDM * TL], BF16)
    P.emit("vector", lambda e: None, waits=[(s_ld, 16), (env["s_cst"], env["cst_target"])])
    P.emit("scalar", lambda e: None, waits=[(s_ld, 16), (env["s_cst"], env["cst_target"])])
    sl4, sl4_t = emit_ln(P, "ln4g", x4, NTC, ln4bf, eps_ap=cc(env, "eps"))
    P.emit("tensor", lambda e: None, waits=[(sl4, sl4_t)])
    st4, st4_t = emit_transpose_16(P, "ln4g", ln4bf, ln4T, env["idn"])

    s_wl = P.sem("s_wl"); s_wc = P.sem("s_wc")
    wbst = P.sb("wbst", [128, 4096], F32)
    wb = P.sb("wb", [128, NDM * 1024], BF16)
    emit_wload(P, env, env["WBLK"], wb, wbst, s_wl, s_wc)

    s_th = P.sem("s_th"); s_ta = P.sem("s_ta"); s_tm = P.sem("s_tm")
    sca = P.sb("sca", [128, 64], F32)
    scmy = P.sb("scmy", [128, NTC], F32)
    ramp = P.sb("ramp", [1, 256], F32)
    stepc = P.sb("stepc", [1, 8], F32)
    locell = P.sb("locell", [1, 8], F32)
    onesrow = P.sb("onesrow", [1, 128], F32)
    ones_f32 = P.sb("ones_g32", [128, 1], F32)
    tgrow = P.sb("tgrow", [1, 256], F32)
    tgb = P.sb("tgb", [128, 256], F32)
    cmpt = P.sb("cmpt", [128, 256], F32)
    cacc = P.sb("cacc", [128, 256], F32)
    cnt = P.sb("cnt", [1, 256], F32)
    fvec = P.sb("fvec", [1, 256], F32)
    mrow = P.sb("mrowg", [1, 8], F32)
    thrc = P.sb("thrc", [128, 1], F32)
    msk = P.sb("msk", [128, NTC], F32)
    pbc = P.ps("pbc", [128, 256], F32)
    pcn = P.ps("pcn", [1, 256], F32)
    for name, dst in [("SCALL", sca), ("SCMY", scmy), ("RAMP1", ramp),
                      ("STEPC", stepc), ("LO0", locell)]:
        P.emit("sync", lambda e, d=env[name], t=dst, n=dst.shape[0]:
               e.dma_start(out=t[0:n, :], in_=d[:, :]),
               incs=[(s_ld, 16)])
    ld_t = P.c(s_ld)
    P.emit("vector", lambda e, t=onesrow: e.memset(t[0:1, :], 1.0), incs=[(s_th, 1)])
    P.emit("vector", lambda e, t=ones_f32: e.memset(t[:, :], 1.0), incs=[(s_th, 1)])
    for r in range(4):
        th0 = P.c(s_th); ta0 = P.c(s_ta); tm0 = P.c(s_tm)
        P.emit("vector", lambda e, t=tgrow, rp=ramp, sc=stepc, lo=locell, r=r:
               e.tensor_scalar(out=t[0:1, :], in0=rp[0:1, :], scalar1=sc[0:1, r:r + 1],
                               scalar2=lo[0:1, r:r + 1], op0=OP.mult, op1=OP.add),
               waits=[(s_ld, ld_t), (s_ta, ta0)], incs=[(s_th, 1)])
        P.emit("tensor", lambda e, ps=pbc, o=onesrow, t=tgrow:
               e.matmul(ps[:, :], o[0:1, :], t[0:1, :], start=True, stop=True),
               waits=[(s_th, th0 + 1), (s_ta, ta0)], incs=[(s_tm, 1)])
        P.emit("scalar", lambda e, ps=pbc, t=tgb:
               e.activation(out=t[:, :], in_=ps[:, :], func=AF.Identity),
               waits=[(s_tm, tm0 + 1)], incs=[(s_ta, 1)])
        for col in range(64):
            if col == 0:
                P.emit("vector", lambda e, t=tgb, s=sca, a=cacc:
                       e.tensor_scalar(out=a[:, :], in0=t[:, :], scalar1=s[:, 0:1],
                                       scalar2=None, op0=OP.is_lt),
                       waits=[(s_ta, ta0 + 1)])
            else:
                P.emit("vector", lambda e, t=tgb, s=sca, c2=cmpt, col=col:
                       e.tensor_scalar(out=c2[:, :], in0=t[:, :], scalar1=s[:, col:col + 1],
                                       scalar2=None, op0=OP.is_lt))
                P.emit("vector", lambda e, a=cacc, c2=cmpt:
                       e.tensor_tensor(out=a[:, :], in0=a[:, :], in1=c2[:, :], op=OP.add),
                       incs=([(s_th, 1)] if col == 63 else []))
        P.emit("tensor", lambda e, ps=pcn, o=ones_f32, a=cacc:
               e.matmul(ps[0:1, :], o[:, 0:1], a[:, :], start=True, stop=True),
               waits=[(s_th, th0 + 2), (s_ta, ta0 + 1)], incs=[(s_tm, 1)])
        P.emit("scalar", lambda e, ps=pcn, t=cnt:
               e.activation(out=t[0:1, :], in_=ps[0:1, :], func=AF.Identity),
               waits=[(s_tm, tm0 + 2)], incs=[(s_ta, 1)])
        P.emit("vector", lambda e, t=cnt, f=fvec:
               e.tensor_scalar(out=f[0:1, :], in0=t[0:1, :], scalar1=6144.0,
                               scalar2=None, op0=OP.is_ge),
               waits=[(s_ta, ta0 + 2)])
        P.emit("vector", lambda e, f=fvec, m=mrow, r=r:
               e.reduce_sum(m[0:1, r:r + 1], f[0:1, :], axis=mybir.AxisListType.X),
               incs=[(s_th, 1)])
        P.emit("scalar", lambda e, m=mrow, lo=locell, sc=stepc, r=r:
               e.activation(out=lo[0:1, r + 1:r + 2], in_=m[0:1, r:r + 1], func=AF.Identity,
                            scale=sc[0:1, r:r + 1], bias=lo[0:1, r:r + 1]),
               waits=[(s_th, th0 + 3)], incs=[(s_ta, 1)])
    ta_f = P.c(s_ta); tm_f = P.c(s_tm)
    P.emit("tensor", lambda e, ps=pbc, o=onesrow, lo=locell:
           e.matmul(ps[:, 0:1], o[0:1, :], lo[0:1, 4:5], start=True, stop=True),
           waits=[(s_ta, ta_f)], incs=[(s_tm, 1)])
    P.emit("scalar", lambda e, ps=pbc, t=thrc:
           e.activation(out=t[:, :], in_=ps[:, 0:1], func=AF.Identity),
           waits=[(s_tm, tm_f + 1)], incs=[(s_ta, 1)])
    P.emit("vector", lambda e, s=scmy, t=thrc, m=msk:
           e.tensor_scalar(out=m[:, :], in0=s[:, :], scalar1=t[:, 0:1], scalar2=None,
                           op0=OP.is_gt),
           waits=[(s_ta, P.c(s_ta)), (s_ld, ld_t)], incs=[(s_th, 1)])
    th_done = P.c(s_th)
    s_gd = P.sem("s_gd")
    if debug:
        P.emit("gpsimd", lambda e, d=o_thr, t=locell: e.dma_start(out=d[:, :], in_=t[0:1, :]),
               waits=[(s_th, th_done)], incs=[(s_gd, 16)])
        P.emit("gpsimd", lambda e, d=o_msk, t=msk: e.dma_start(out=d[:, :], in_=t[:, :]),
               incs=[(s_gd, 16)])

    s_gm = P.sem("s_gm"); s_gev = P.sem("s_gev"); s_gch = P.sem("s_gch")
    s_out = P.sem("s_out")
    pgl = [P.ps(f"pgl{i}", [128, 512], F32) for i in range(2)]
    zfb = P.sb("zfb", [128, 1024], F32)
    z2b = P.sb("z2b", [128, 1024], F32)
    gob = P.sb("gob", [128, 1024], F32)
    ostg = [P.sb(f"ostg{i}", [128, 1024], F32) for i in range(2)]
    gidx = 0
    for c in range(NTC):
        cp = c % 2
        for nch in range(2):
            for dm in range(NDM):
                w = []
                if dm == 0:
                    w = [(st4, st4_t), (s_wc, 2)]
                    if gidx >= 2:
                        w.append((s_gev, gidx - 1))
                P.emit("tensor", lambda e, ps=pgl[gidx % 2], lt=ln4T, cb=wb, c=c, nch=nch, dm=dm:
                       e.matmul(ps[:, :], lt[:, dm * TL + c * 128:dm * TL + (c + 1) * 128],
                                cb[:, dm * 1024 + nch * 512:dm * 1024 + (nch + 1) * 512],
                                start=(dm == 0), stop=(dm == NDM - 1)),
                       waits=w, incs=([(s_gm, 1)] if dm == NDM - 1 else []))
            P.emit("vector", lambda e, ps=pgl[gidx % 2], t=zfb, nch=nch:
                   e.tensor_copy(out=t[:, nch * 512:(nch + 1) * 512], in_=ps[:, :]),
                   waits=[(s_gm, gidx + 1)], incs=[(s_gev, 1)])
            gidx += 1
        ch0 = P.c(s_gch)
        P.emit("vector", lambda e, z=zfb, t=z2b:
               e.tensor_tensor(out=t[:, :], in0=z[:, :], in1=z[:, :], op=OP.mult),
               waits=[(s_gev, 2 * (c + 1))])
        P.emit("vector", lambda e, t=z2b:
               e.tensor_scalar(out=t[:, :], in0=t[:, :], scalar1=GELU_A, scalar2=1.0,
                               op0=OP.mult, op1=OP.add))
        P.emit("vector", lambda e, z=zfb, t=z2b, g=gob:
               e.tensor_tensor(out=g[:, :], in0=t[:, :], in1=z[:, :], op=OP.mult),
               incs=[(s_gch, 1)])
        P.emit("scalar", lambda e, g=gob, t=z2b:
               e.activation(out=t[:, :], in_=g[:, :], func=AF.Tanh, scale=GELU_C),
               waits=[(s_gch, ch0 + 1)], incs=[(s_gch, 1)])
        P.emit("vector", lambda e, t=z2b:
               e.tensor_scalar(out=t[:, :], in0=t[:, :], scalar1=1.0, scalar2=0.5,
                               op0=OP.add, op1=OP.mult),
               waits=[(s_gch, ch0 + 2)])
        P.emit("vector", lambda e, z=zfb, t=z2b, g=gob:
               e.tensor_tensor(out=g[:, :], in0=t[:, :], in1=z[:, :], op=OP.mult))
        sl = slice(c * 1024, (c + 1) * 1024)
        P.emit("vector", lambda e, g=gob, l=ln4bf, t=z2b, sl=sl:
               e.tensor_tensor(out=t[:, :], in0=g[:, :], in1=l[:, sl], op=OP.subtract),
               waits=[(s_th, th_done)])
        P.emit("vector", lambda e, t=z2b, m=msk, c=c:
               e.tensor_scalar(out=t[:, :], in0=t[:, :], scalar1=m[:, c:c + 1], scalar2=None,
                               op0=OP.mult))
        P.emit("vector", lambda e, x=x4, l=ln4bf, o=ostg[cp], sl=sl:
               e.tensor_tensor(out=o[:, :], in0=x[:, sl], in1=l[:, sl], op=OP.add),
               waits=([(s_out, 16 * (c // 2))] if c >= 2 else []))
        P.emit("vector", lambda e, t=z2b, o=ostg[cp]:
               e.tensor_tensor(out=o[:, :], in0=o[:, :], in1=t[:, :], op=OP.add),
               incs=[(s_gch, 1)])
        P.emit("gpsimd", lambda e, d=OUT, t=ostg[cp], c=c:
               e.dma_start(out=d[c * 128:(c + 1) * 128, :], in_=t[:, :]),
               waits=[(s_gch, P.c(s_gch))], incs=[(s_out, 16)])
    fw = [(s_out, P.c(s_out))]
    if debug:
        fw.append((s_gd, P.c(s_gd)))
    P.emit("gpsimd", lambda e: None, waits=fw)
    P.finalize()
    return nc, P


def _stageB_maps(inp, resA):
    maps = []
    lnT_full = {}
    for b in range(B):
        lnT_full[b] = np.concatenate([np.asarray(resA[4 * b + g]["o_lnT"]) for g in range(G)],
                                     axis=1)
    for c in range(NCORE):
        b, g = c // G, c % G
        E = slice(g * MYE, (g + 1) * MYE)
        heads = slice(4 * g, 4 * g + 4)
        m = {
            "LNT": lnT_full[b],
            "LNTG": np.ascontiguousarray(lnT_full[b][E.start:E.stop, :]),
            "TAB": np.asarray(resA[c]["o_tab"]),
            "QWT": np.ascontiguousarray(inp["L0_attn_q_w"].T[:, E]),
            "KWT": np.ascontiguousarray(inp["L0_attn_k_w"].T[:, E]),
            "VWT": np.ascontiguousarray(inp["L0_attn_v_w"].T[:, E]),
            "QB": np.ascontiguousarray(inp["L0_attn_q_b"][E]),
            "KB": np.ascontiguousarray(inp["L0_attn_k_b"][E]),
            "VB": np.ascontiguousarray(inp["L0_attn_v_b"][E]),
            "GWT": np.ascontiguousarray(inp["L0_attn_gru_w"].T),
            "GB": np.ascontiguousarray(inp["L0_attn_gru_b"]),
            "GC": np.ascontiguousarray(inp["L0_attn_gru_const"].reshape(H)[heads]).reshape(MYH, 1),
            "OWT": np.ascontiguousarray(inp["L0_attn_out_w"].T[E, :]),
            "CONSTS": CONSTS_H,
            "IDN": IDN_H,
            "ONES": ONES_H,
        }
        maps.append(m)
    return maps


def _core_slices(inp):
    x = inp["L0_attn_hidden_states"]
    oh = _onehot()
    relp = np.zeros((384, 16), np.float32)
    relp[0:320, :] = inp["L0_attn_rel_embed"]
    maps = []
    for c in range(NCORE):
        b, g = c // G, c % G
        heads = list(range(4 * g, 4 * g + 4))
        m = {
            "X": np.ascontiguousarray(x[b, g * TL:(g + 1) * TL, :]),
            "RELP": np.ascontiguousarray(relp[:, heads]),
            "OH": oh,
            "CONSTS": CONSTS_H,
            "IDN": IDN_H,
            "ONES": ONES_H,
        }
        maps.append(m)
    return maps


RAMP1_H = np.arange(1, 257, dtype=np.float32).reshape(1, 256)
STEPC_H = np.array([[512.0 / 257.0 ** (r + 1) for r in range(4)] + [0, 0, 0, 0]],
                   np.float32)
LO0_H = np.array([[-256.0, 0, 0, 0, 0, 0, 0, 0]], np.float32)


def _stageC_maps(inp, rB):
    maps = []
    for c in range(NCORE):
        b, g = c // G, c % G
        rows = slice(g * TL, (g + 1) * TL)
        m = {
            "X": np.ascontiguousarray(inp["L0_attn_hidden_states"][b, rows, :]),
            "OUTB": np.ascontiguousarray(inp["L0_attn_out_b"]),
            "CBT": np.ascontiguousarray(inp["L0_ffn_CB"].T),
            "SCB": np.ascontiguousarray(inp["L0_ffn_SCB"]),
            "FBIAS": np.ascontiguousarray(inp["L0_ffn_bias"]),
            "CONSTS": CONSTS_H, "IDN": IDN_H, "ONES": ONES_H,
        }
        for i in range(4):
            m[f"OP{i}"] = np.ascontiguousarray(np.asarray(rB[4 * b + i]["OPART"])[rows, :])
        maps.append(m)
    return maps


def _stageD_maps(inp, rC):
    W1 = inp["L1_attn_W1"]
    b1 = inp["L1_attn_b1"]
    lnt_full = {}
    for b in range(B):
        lnt_full[b] = np.concatenate([np.asarray(rC[4 * b + g]["LNT3"]) for g in range(G)],
                                     axis=1)
    maps = []
    for c in range(NCORE):
        b, g = c // G, c % G
        E = slice(g * MYE, (g + 1) * MYE)
        m = {
            "LNT": lnt_full[b],
            "W1MY": np.ascontiguousarray(
                np.concatenate([W1[:, t * D + E.start:t * D + E.stop] for t in range(4)],
                               axis=1)),
            "B1MY": np.ascontiguousarray(
                np.concatenate([b1[t * D + E.start:t * D + E.stop] for t in range(4)])),
            "CONSTS": CONSTS_H, "IDN": IDN_H, "ONES": ONES_H,
        }
        maps.append(m)
    return maps


def _stageE_maps(inp, rD):
    maps = []
    for c in range(NCORE):
        b, g = c // G, c % G
        E = slice(g * MYE, (g + 1) * MYE)
        m = {
            "TTD": np.asarray(rD[c]["TTD"]),
            "SS4": np.concatenate([np.asarray(rD[4 * b + j]["STS"]) for j in range(4)], axis=0),
            "SQ4": np.concatenate([np.asarray(rD[4 * b + j]["STQ"]) for j in range(4)], axis=0),
            "GAM": np.ascontiguousarray(inp["L1_attn_gamma"][E]),
            "BET": np.ascontiguousarray(inp["L1_attn_beta"][E]),
            "W2MY": np.ascontiguousarray(inp["L1_attn_W2"][E, :]),
            "CONSTS": CONSTS_H, "IDN": IDN_H, "ONES": ONES_H,
        }
        maps.append(m)
    return maps


def _stageF_maps(inp, rC, rE):
    maps = []
    for c in range(NCORE):
        b, g = c // G, c % G
        rows = slice(g * TL, (g + 1) * TL)
        m = {
            "X3": np.asarray(rC[c]["X3"]),
            "B2": np.ascontiguousarray(inp["L1_attn_b2"]),
            "WR": np.ascontiguousarray(inp["L1_ffn_w_router"]),
            "CONSTS": CONSTS_H, "IDN": IDN_H, "ONES": ONES_H,
        }
        for i in range(4):
            m[f"WP{i}"] = np.ascontiguousarray(np.asarray(rE[4 * b + i]["WPART"])[rows, :])
        maps.append(m)
    return maps


def _stageG_maps(inp, rF):
    maps = []
    for c in range(NCORE):
        b, g = c // G, c % G
        m = {
            "X4": np.asarray(rF[c]["X4"]),
            "SCALL": np.concatenate([np.asarray(rF[4 * b + j]["SCL"]) for j in range(4)],
                                    axis=1),
            "SCMY": np.asarray(rF[c]["SCL"]),
            "WBLK": np.ascontiguousarray(inp["L1_ffn_w_block"]),
            "RAMP1": RAMP1_H, "STEPC": STEPC_H, "LO0": LO0_H,
            "CONSTS": CONSTS_H, "IDN": IDN_H, "ONES": ONES_H,
        }
        maps.append(m)
    return maps


def kernel(**inp):
    cores = list(range(NCORE))

    def run(builder, maps, **bkw):
        nc, P = builder(**bkw)
        try:
            res = None
            for attempt in range(3):
                try:
                    res = run_bass_kernel_spmd(nc, maps, core_ids=cores)
                    break
                except Exception:
                    if attempt == 2:
                        raise
            results = res.results
        except Exception:
            lo = run_bass_kernel_spmd(nc, maps[:4], core_ids=cores[:4])
            hi = run_bass_kernel_spmd(nc, maps[4:], core_ids=cores[:4])
            results = lo.results + hi.results
        P.close()
        return results

    rA = run(build_stageA, _core_slices(inp))
    rB = run(build_stageB, _stageB_maps(inp, rA))
    rC = run(build_stageC, _stageC_maps(inp, rB))
    rD = run(build_stageD, _stageD_maps(inp, rC))
    rE = run(build_stageE, _stageE_maps(inp, rD))
    rF = run(build_stageF, _stageF_maps(inp, rC, rE))
    rG = run(build_stageG, _stageG_maps(inp, rF))
    out = np.zeros((B, T, D), np.float32)
    for c in range(NCORE):
        b, g = c // G, c % G
        out[b, g * TL:(g + 1) * TL, :] = np.asarray(rG[c]["OUT"])
    return out
